# version 02
# baseline (speedup 1.0000x reference)
import numpy as np
import ml_dtypes
import concourse.bass as bass
import concourse.mybir as mybir
from concourse.bass_utils import run_bass_kernel_spmd
from contextlib import ExitStack

F32 = mybir.dt.float32
BF16 = mybir.dt.bfloat16
AF = mybir.ActivationFunctionType
ALU = mybir.AluOpType
AX = mybir.AxisListType

D = 1024
H = 16
DH = 64
NPAIR = 8
NE = 8
I_FF = 2048
EPS = 1e-5
B_CORE = 2
NCORES = 8
NEG = -1.0e30

DMA_CH = ("DS", "DG", "DA")
CH_ENGINE = {"T": "tensor", "V": "vector", "S": "scalar", "G": "gpsimd",
             "SY": "sync", "DS": "sync", "DG": "gpsimd", "DA": "scalar"}


class Prog:

    def __init__(self, nc, es):
        self.nc = nc
        self.ops = []
        self.count = {ch: 0 for ch in CH_ENGINE}
        self.sems = {ch: es.enter_context(nc.semaphore("sem_" + ch))
                     for ch in CH_ENGINE if ch != "SY"}
        self.seen = {}

    def op(self, ch, fn, waits=(), sig=False, force=()):
        waits = [w for w in waits if w is not None]
        force = [w for w in force if w is not None]
        for (wc, wv) in waits + force:
            assert wv <= self.count[wc], (wc, wv, self.count[wc])
        inc = 16 if ch in DMA_CH else (1 if sig else 0)
        self.ops.append((ch, fn, tuple(waits), inc, tuple(force)))
        if inc:
            self.count[ch] += inc
            return (ch, self.count[ch])
        return None

    def dma(self, ch, out, in_, waits=()):
        assert ch in DMA_CH
        return self.op(ch, lambda eng, o=out, i=in_: eng.dma_start(out=o, in_=i),
                       waits=waits)

    def join_cv(self):
        return [(c, self.count[c]) for c in ("T", "V", "S") if self.count[c] > 0]

    def flush(self, block):
        by_eng = {}
        for rec in self.ops:
            by_eng.setdefault(CH_ENGINE[rec[0]], []).append(rec)
        self.ops = []
        for eng_name in ("sync", "gpsimd", "tensor", "vector", "scalar"):
            recs = by_eng.get(eng_name)
            if not recs:
                continue
            seen = self.seen.setdefault(eng_name, {c: 0 for c in CH_ENGINE})

            def emit(eng, recs=recs, seen=seen, eng_name=eng_name):
                for ch, fn, waits, inc, force in recs:
                    for wc, wv in waits:
                        if wc not in DMA_CH and CH_ENGINE[wc] == eng_name:
                            continue
                        if wv <= seen[wc]:
                            continue
                        eng.wait_ge(self.sems[wc], wv)
                        seen[wc] = wv
                    for wc, wv in force:
                        eng.wait_ge(self.sems[wc], wv)
                    inst = fn(eng)
                    if inc:
                        inst.then_inc(self.sems[ch], inc)

            getattr(block, eng_name)(emit)


def build_core_program(S=4096, debug=False, debug_stage=None):
    NT = S // 128
    NQC = S // 512
    NKT = S // 128
    TOK = B_CORE * S
    NTT = TOK // 128
    DEN_W = min(1024, S)
    NPD = S // DEN_W
    ETOK = min(1024, TOK)
    NEI = TOK // ETOK
    ETT = ETOK // 128
    ECH = ETOK // 512
    assert NT % 4 == 0

    nc = bass.Bass("TRN2", target_bir_lowering=False)
    dt = nc.dram_tensor
    _uid = [0]

    def _nm(s):
        _uid[0] += 1
        return f"{s}_{_uid[0]}"

    xin = dt("xin", [B_CORE, S, D], F32, kind="ExternalInput").ap()
    wpk_in = dt("wpk_in", [NPAIR, 128, 8 * 384], F32, kind="ExternalInput").ap()
    woutpk = dt("woutpk", [128, 8 * D], F32, kind="ExternalInput").ap()
    routpk = dt("routpk", [128, 8 * 8], F32, kind="ExternalInput").ap()
    rows_in = dt("rows_in", [3, D], F32, kind="ExternalInput").ap()
    ctab_in = dt("ctab_in", [128, S], BF16, kind="ExternalInput").ap()
    stab_in = dt("stab_in", [128, S], BF16, kind="ExternalInput").ap()
    masks_in = dt("masks_in", [128, 4 * 512], BF16, kind="ExternalInput").ap()
    identb_in = dt("identb_in", [128, 128], BF16, kind="ExternalInput").ap()
    identf_in = dt("identf_in", [128, 128], F32, kind="ExternalInput").ap()
    gate_in = dt("gate_in", [NE + 1, D, I_FF], F32, kind="ExternalInput").ap()
    up_in = dt("up_in", [NE + 1, D, I_FF], F32, kind="ExternalInput").ap()
    down_in = dt("down_in", [NE + 1, I_FF, D], F32, kind="ExternalInput").ap()

    out = dt("out", [B_CORE, S, D], F32, kind="ExternalOutput").ap()

    y_scr = dt("y_scr", [NPAIR, 128, S], BF16, kind="Internal").ap()
    x1_scr = dt("x1_scr", [TOK, D], F32, kind="Internal").ap()
    xfn_scr = dt("xfn_scr", [8, 128, TOK], BF16, kind="Internal").ap()
    routed_scr = dt("routed_scr", [TOK, D], BF16, kind="Internal").ap()
    we_scr = dt("we_scr", [8, TOK], F32, kind="Internal").ap()
    den_scr = dt("den_scr", [2, S], F32, kind="Internal").ap()
    if debug:
        dbg_x1 = dt("dbg_x1", [TOK, D], F32, kind="ExternalOutput").ap()
        dbg_y = dt("dbg_y", [NPAIR, 128, S], BF16, kind="ExternalOutput").ap()
        dbg_xfn = dt("dbg_xfn", [8, 128, TOK], BF16, kind="ExternalOutput").ap()
        dbg_we = dt("dbg_we", [8, TOK], F32, kind="ExternalOutput").ap()
        dbg_rt = dt("dbg_rt", [TOK, D], BF16, kind="ExternalOutput").ap()
        dbg_den = dt("dbg_den", [2, S], F32, kind="ExternalOutput").ap()

    with ExitStack() as ctx:
        p = Prog(nc, ctx)
        sbuf = lambda n, sh, d_: ctx.enter_context(nc.sbuf_tensor(n, sh, d_))
        block = ctx.enter_context(nc.Block())
        dbg_bufs = {}

        def dbg_dump(name, src_ap, waits):
            shp = list(src_ap.shape)
            dtp = src_ap.dtype
            t = nc.dram_tensor("dump_" + name, shp, dtp, kind="ExternalOutput").ap()
            p.dma("DS", t, src_ap, waits=waits)

        def dbg_finish():
            p.op("SY", lambda eng: eng.nop(),
                 waits=[(ch, p.count[ch]) for ch in DMA_CH if p.count[ch] > 0])
            p.flush(block)

        identb = sbuf("identb", [128, 128], BF16)
        identf = sbuf("identf", [128, 128], F32)
        ones_f = sbuf("ones_f", [1, 128], F32)
        row_tmp = sbuf("row_tmp", [1, D], F32)
        bout_bc = sbuf("bout_bc", [128, D], F32)
        lnw_bc = sbuf("lnw_bc", [128, D], F32)
        lnb_bc = sbuf("lnb_bc", [128, D], F32)
        routsb = sbuf("routsb", [128, 8 * 8], F32)

        ld_const = [p.dma("DS", t[:], s) for t, s in
                    ((identb, identb_in), (identf, identf_in), (routsb, routpk))]
        ev_ones = p.op("V", lambda eng: eng.memset(ones_f[:], 1.0), sig=True)
        eps_sb = sbuf("eps_sb", [128, 1], F32)
        ev_eps = p.op("V", lambda eng: eng.memset(eps_sb[:], EPS), sig=True)

        with nc.psum_tensor(_nm("bcpre_ps"), [128, 512], F32) as bcpre_ps:
            prev = None
            for ri, dst in ((0, bout_bc), (1, lnw_bc), (2, lnb_bc)):
                evr = p.dma("DS", row_tmp[:], rows_in[ri:ri + 1, :],
                            waits=[prev])
                for c in range(D // 512):
                    evm = p.op("T", lambda eng, c=c: eng.matmul(
                        bcpre_ps[:], ones_f[0:1, :],
                        row_tmp[0:1, c * 512:(c + 1) * 512],
                        start=True, stop=True),
                        waits=[evr, ev_ones, prev], sig=True)
                    prev = p.op("V", lambda eng, c=c, dst=dst: eng.tensor_copy(
                        dst[:, c * 512:(c + 1) * 512], bcpre_ps[:]),
                        waits=[evm], sig=True)
            p.flush(block)

        with ExitStack() as c1:
            sb1 = lambda n, sh, d_: c1.enter_context(nc.sbuf_tensor(n, sh, d_))
            xT = sb1("xT", [128, 8 * S], BF16)
            xnat_b = sb1("xnat_b", [128, 2 * D], BF16)
            qT = sb1("qT", [128, S], BF16)
            kT = sb1("kT", [128, S], BF16)
            qr_ch = sb1("qr_ch", [128, 2 * 512], BF16)
            krT = sb1("krT", [128, S], BF16)
            qs_tmp = sb1("qs_tmp", [128, 512], BF16)
            rope_tmp = sb1("rope_tmp", [128, 512], BF16)
            v_sb = sb1("v_sb", [128, S], BF16)
            y_stage = sb1("y_stage", [128, S], BF16)
            exp_sb = sb1("exp_sb", [128, 2 * 1024], BF16)
            trash_b = sb1("trash_b", [128, max(DEN_W, D)], BF16)
            den_acc = sb1("den_acc", [128, NPD * 2 * NKT], F32)
            den_sum = sb1("den_sum", [128, 2 * NKT], F32)
            den_invT = sb1("den_invT", [64, 128], F32)
            den_row_ch = sb1("den_row_ch", [1, 2 * 512], F32)
            bc_sb = sb1("bc_sb", [128, 2 * 512], F32)
            wpk_b = sb1("wpk_b", [128, 2 * 3072], BF16)
            wout_sb = sb1("wout_sb", [128, 8 * D], BF16)
            yst_sb = sb1("yst_sb", [128, 2 * 512], BF16)
            x_f32 = sb1("x_f32", [128, D], F32)
            t_f32 = sb1("t_f32", [128, D], F32)
            stat = sb1("stat", [128, 8], F32)
            lndbg = sb1("lndbg", [128, 3 * 32], F32)
            ctab = sb1("ctab", [128, S], BF16)
            stab = sb1("stab", [128, S], BF16)
            masks = sb1("masks", [128, 4 * 512], BF16)

            ld_tab = [p.dma("DS", ctab[:], ctab_in),
                      p.dma("DS", stab[:], stab_in),
                      p.dma("DS", masks[:], masks_in)]
            ld_const = ld_const + ld_tab
            ev_woutcast = p.dma("DG", wout_sb[:], woutpk)

            x1_done = []
            bank_gate = []
            wpkb_free = [None, None]
            for b in range(B_CORE):
                with nc.psum_tensor(_nm("trb_ps"), [128, 2048], BF16) as trb_ps:
                    xnat_free = [None, None]
                    tr_free = [None, None]
                    ev_xT_last = None
                    for tt in range(NT):
                        sl = tt % 2
                        evc = p.dma("DG", xnat_b[:, sl * D:(sl + 1) * D],
                                    xin[b, tt * 128:(tt + 1) * 128, :],
                                    waits=[xnat_free[sl]] + ld_const)
                        for c in range(8):
                            ts = c % 2
                            evt = p.op("T", lambda eng, sl=sl, c=c, ts=ts: eng.transpose(
                                trb_ps[:, ts * 1024: ts * 1024 + 128],
                                xnat_b[:, sl * D + c * 128: sl * D + (c + 1) * 128],
                                identb[:]),
                                waits=[evc, tr_free[ts]] + bank_gate, sig=True)
                            eve = p.op("V", lambda eng, c=c, tt=tt, ts=ts: eng.tensor_copy(
                                xT[:, c * S + tt * 128: c * S + (tt + 1) * 128],
                                trb_ps[:, ts * 1024: ts * 1024 + 128]),
                                waits=[evt], sig=True)
                            tr_free[ts] = eve
                            if c == 7:
                                xnat_free[sl] = eve
                            ev_xT_last = eve
                    bank_gate = [ev_xT_last]
                    if debug_stage == "A1" and b == 0:
                        dbg_dump("xT", xT[:], p.join_cv())
                        dbg_finish()
                        return nc
                    p.flush(block)

                with ExitStack() as cps:
                    den_ps = cps.enter_context(nc.psum_tensor(_nm("den_ps"), [128, 2 * DEN_W], F32))
                    num_ps = cps.enter_context(nc.psum_tensor(_nm("num_ps"), [128, 1024], F32))
                    yT_ps = cps.enter_context(nc.psum_tensor(_nm("yT_ps"), [128, 512], F32))
                    bc_ps = cps.enter_context(nc.psum_tensor(_nm("bc_ps"), [128, 512], F32))

                    wpkf_free = None
                    y_stage_free = None
                    qr_slot_free = [None, None]
                    den_row_free = None
                    den_slot_ev = [None, None]
                    exp_slot_ev = [None, None]
                    yT_free = None
                    bc_free = None
                    num_free = None
                    dtr_free = None

                    for pr in range(NPAIR):
                        psl = pr % 2
                        evw = p.dma("DG", wpk_b[:, psl * 3072:(psl + 1) * 3072],
                                    wpk_in[pr, :, :], waits=[wpkb_free[psl]])

                        ser_free = [den_slot_ev[0], den_slot_ev[1]]
                        q_ready = k_ready = None
                        for colo, dstbuf in ((0, qT), (128, kT)):
                            for cc in range(NQC):
                                ssl = cc % 2
                                ev = None
                                for k in range(8):
                                    ev = p.op("T", lambda eng, k=k, cc=cc, ssl=ssl, colo=colo, psl=psl: eng.matmul(
                                        den_ps[:, ssl * DEN_W: ssl * DEN_W + 512],
                                        wpk_b[:, psl * 3072 + k * 384 + colo: psl * 3072 + k * 384 + colo + 128],
                                        xT[:, k * S + cc * 512: k * S + cc * 512 + 512],
                                        start=(k == 0), stop=(k == 7)),
                                        waits=([evw, ser_free[ssl], ev_xT_last] + bank_gate) if k == 0 else (),
                                        sig=(k == 7))
                                eve = p.op("V", lambda eng, cc=cc, ssl=ssl, dstbuf=dstbuf: eng.tensor_copy(
                                    dstbuf[:, cc * 512:(cc + 1) * 512],
                                    den_ps[:, ssl * DEN_W: ssl * DEN_W + 512]),
                                    waits=[ev], sig=True)
                                ser_free[ssl] = eve
                            if colo == 0:
                                q_ready = eve
                            else:
                                k_ready = eve
                        for g in range(NKT // 4):
                            ssl = g % 2
                            ev = None
                            first = True
                            for tsub in range(4):
                                tt = g * 4 + tsub
                                for k in range(8):
                                    ev = p.op("T", lambda eng, k=k, tt=tt, tsub=tsub, ssl=ssl, psl=psl: eng.matmul(
                                        den_ps[:, ssl * DEN_W + tsub * 128: ssl * DEN_W + (tsub + 1) * 128],
                                        xT[:, k * S + tt * 128: k * S + (tt + 1) * 128],
                                        wpk_b[:, psl * 3072 + k * 384 + 256: psl * 3072 + k * 384 + 384],
                                        start=(tsub == 0 and k == 0), stop=(tsub == 3 and k == 7),
                                        skip_group_check=True),
                                        waits=[evw, ser_free[ssl]] if first else (),
                                        sig=(tsub == 3 and k == 7))
                                    first = False
                            eve = p.op("V", lambda eng, g=g, ssl=ssl: eng.tensor_copy(
                                v_sb[:, g * 512:(g + 1) * 512],
                                den_ps[:, ssl * DEN_W: ssl * DEN_W + 512]),
                                waits=[ev], sig=True)
                            ser_free[ssl] = eve
                        v_ready = eve
                        wpkb_free[psl] = eve
                        den_slot_ev = [ser_free[0], ser_free[1]]

                        ev_ropek = None
                        for cc in range(NQC):
                            evsh = p.op("V", lambda eng, cc=cc: eng.stream_shuffle(
                                qs_tmp[:], kT[:, cc * 512:(cc + 1) * 512],
                                [i ^ 1 for i in range(32)]), waits=[k_ready],
                                force=[k_ready], sig=True)
                            p.op("V", lambda eng, cc=cc: eng.tensor_mul(
                                krT[:, cc * 512:(cc + 1) * 512],
                                ctab[:, cc * 512:(cc + 1) * 512],
                                kT[:, cc * 512:(cc + 1) * 512]))
                            p.op("V", lambda eng, cc=cc: eng.tensor_mul(
                                rope_tmp[:], stab[:, cc * 512:(cc + 1) * 512],
                                qs_tmp[:]), force=[evsh])
                            ev_ropek = p.op("V", lambda eng, cc=cc: eng.tensor_add(
                                krT[:, cc * 512:(cc + 1) * 512],
                                krT[:, cc * 512:(cc + 1) * 512],
                                rope_tmp[:]), sig=True)

                        den_units = [(h, qt, pc) for h in range(2)
                                     for qt in range(NKT) for pc in range(NPD)]
                        num_units = [(cq, kt) for cq in range(NQC)
                                     for kt in range(4 * cq + 4)]
                        nd, nn = len(den_units), len(num_units)
                        di = 0
                        y_evac = {}
                        qr_ev = [None, None]
                        for j, (cq, kt) in enumerate(num_units):
                            qsl = cq % 2
                            if kt == 0:
                                evsh = p.op("V", lambda eng, cq=cq, qsl=qsl: eng.stream_shuffle(
                                    qs_tmp[:], qT[:, cq * 512:(cq + 1) * 512],
                                    [i ^ 1 for i in range(32)]),
                                    waits=[q_ready, qr_slot_free[qsl]],
                                    force=[q_ready], sig=True)
                                p.op("V", lambda eng, cq=cq, qsl=qsl: eng.tensor_mul(
                                    qr_ch[:, qsl * 512:(qsl + 1) * 512],
                                    ctab[:, cq * 512:(cq + 1) * 512],
                                    qT[:, cq * 512:(cq + 1) * 512]))
                                p.op("V", lambda eng, cq=cq: eng.tensor_mul(
                                    rope_tmp[:], stab[:, cq * 512:(cq + 1) * 512],
                                    qs_tmp[:]), force=[evsh])
                                qr_ev[qsl] = p.op("V", lambda eng, qsl=qsl: eng.tensor_add(
                                    qr_ch[:, qsl * 512:(qsl + 1) * 512],
                                    qr_ch[:, qsl * 512:(qsl + 1) * 512],
                                    rope_tmp[:]), sig=True)
                            dlim = ((j + 1) * nd) // nn
                            while di < dlim:
                                h, qt, pc = den_units[di]
                                dsl = di % 2
                                ev = None
                                for half in range(DEN_W // 512):
                                    kc = pc * (DEN_W // 512) + half
                                    ev = p.op("T", lambda eng, h=h, qt=qt, kc=kc, half=half, dsl=dsl: eng.matmul(
                                        den_ps[:, dsl * DEN_W + half * 512: dsl * DEN_W + (half + 1) * 512],
                                        qT[64 * h: 64 * h + 64, qt * 128:(qt + 1) * 128],
                                        kT[64 * h: 64 * h + 64, kc * 512:(kc + 1) * 512],
                                        start=True, stop=True),
                                        waits=[q_ready, k_ready, den_slot_ev[dsl]] if half == 0 else (),
                                        sig=(half == DEN_W // 512 - 1))
                                col = pc * (2 * NKT) + h * NKT + qt
                                evs = p.op("S", lambda eng, col=col, dsl=dsl: eng.activation(
                                    trash_b[:, 0:DEN_W], den_ps[:, dsl * DEN_W:(dsl + 1) * DEN_W],
                                    AF.Exp, scale=0.125,
                                    accum_out=den_acc[:, col:col + 1]),
                                    waits=[ev], sig=True)
                                den_slot_ev[dsl] = evs
                                di += 1
                            esl = j % 2
                            diag = kt >= 4 * cq
                            ev = None
                            for h in range(2):
                                ev = p.op("T", lambda eng, h=h, kt=kt, qsl=qsl: eng.matmul(
                                    num_ps[:, h * 512:(h + 1) * 512],
                                    krT[64 * h:64 * h + 64, kt * 128:(kt + 1) * 128],
                                    qr_ch[64 * h:64 * h + 64, qsl * 512:(qsl + 1) * 512],
                                    start=True, stop=not diag, skip_group_check=True),
                                    waits=[ev_ropek, qr_ev[qsl], num_free] if h == 0 else (),
                                    sig=(h == 1 and not diag))
                                if diag:
                                    tq = kt - 4 * cq
                                    ev = p.op("T", lambda eng, h=h, tq=tq: eng.matmul(
                                        num_ps[:, h * 512:(h + 1) * 512],
                                        identb[:], masks[:, tq * 512:(tq + 1) * 512],
                                        start=False, stop=True, skip_group_check=True),
                                        sig=(h == 1))
                            evex = p.op("S", lambda eng, esl=esl: eng.activation(
                                exp_sb[:, esl * 1024:(esl + 1) * 1024], num_ps[:],
                                AF.Exp, scale=0.125),
                                waits=[ev, exp_slot_ev[esl]], sig=True)
                            num_free = evex
                            lastkt = 4 * cq + 3
                            evav = None
                            for h in range(2):
                                evav = p.op("T", lambda eng, h=h, kt=kt, esl=esl, lastkt=lastkt: eng.matmul(
                                    yT_ps[64 * h:64 * h + 64, :],
                                    v_sb[:, kt * 128 + h * 64: kt * 128 + h * 64 + 64],
                                    exp_sb[:, esl * 1024 + h * 512: esl * 1024 + (h + 1) * 512],
                                    start=(kt == 0), stop=(kt == lastkt),
                                    skip_group_check=True),
                                    waits=([evex, v_ready] + ([yT_free] if kt == 0 else [])) if h == 0 else (),
                                    sig=(h == 1))
                            exp_slot_ev[esl] = evav
                            if kt == lastkt:
                                qr_slot_free[qsl] = ev
                                evy = p.op("V", lambda eng, cq=cq: eng.tensor_copy(
                                    y_stage[:, cq * 512:(cq + 1) * 512], yT_ps[:]),
                                    waits=[evav, y_stage_free], sig=True)
                                yT_free = evy
                                y_evac[cq] = evy
                        evp = None
                        if NPD == 1:
                            evp = p.op("V", lambda eng: eng.tensor_copy(
                                den_sum[:], den_acc[:, 0:2 * NKT]),
                                waits=[den_slot_ev[0], den_slot_ev[1]], sig=True)
                        else:
                            evp = p.op("V", lambda eng: eng.tensor_add(
                                den_sum[:], den_acc[:, 0:2 * NKT],
                                den_acc[:, 2 * NKT:4 * NKT]),
                                waits=[den_slot_ev[0], den_slot_ev[1]], sig=True)
                            for pc in range(2, NPD):
                                evp = p.op("V", lambda eng, pc=pc: eng.tensor_add(
                                    den_sum[:], den_sum[:],
                                    den_acc[:, pc * 2 * NKT:(pc + 1) * 2 * NKT]),
                                    sig=True)
                        evr = p.op("V", lambda eng: eng.reciprocal(
                            den_sum[:], den_sum[:]), waits=[evp], force=[evp],
                            sig=True)
                        evtr = p.op("T", lambda eng: eng.transpose(
                            bc_ps[0:2 * NKT, 0:128], den_sum[:], identf[:]),
                            waits=[evr, bc_free, dtr_free], sig=True)
                        evte = p.op("V", lambda eng: eng.tensor_copy(
                            den_invT[0:2 * NKT, :], bc_ps[0:2 * NKT, 0:128]),
                            waits=[evtr], sig=True)
                        evrows = p.dma(
                            "DS",
                            den_scr.rearrange("h (t n) -> (h t) n", n=128),
                            den_invT[0:2 * NKT, :], waits=[evte])
                        for cq in range(NQC):
                            evdl = [p.dma("DS", den_row_ch[0:1, h * 512:(h + 1) * 512],
                                          den_scr[h, cq * 512:(cq + 1) * 512][None, :],
                                          waits=[evrows, den_row_free])
                                    for h in range(2)]
                            evbc = None
                            for h in range(2):
                                evbc = p.op("T", lambda eng, h=h: eng.matmul(
                                    bc_ps[64 * h: 64 * h + 64, :],
                                    ones_f[0:1, 0:64],
                                    den_row_ch[0:1, h * 512:(h + 1) * 512],
                                    start=True, stop=True, skip_group_check=True),
                                    waits=[evdl[1], bc_free] if h == 0 else (),
                                    sig=(h == 1))
                            den_row_free = evbc
                            evbe = p.op("V", lambda eng, cq=cq: eng.tensor_copy(
                                bc_sb[:, (cq % 2) * 512:(cq % 2) * 512 + 512],
                                bc_ps[:]), waits=[evbc], sig=True)
                            bc_free = evbe
                            dtr_free = evbe
                            evsc = p.op("V", lambda eng, cq=cq: eng.tensor_mul(
                                y_stage[:, cq * 512:(cq + 1) * 512],
                                y_stage[:, cq * 512:(cq + 1) * 512],
                                bc_sb[:, (cq % 2) * 512:(cq % 2) * 512 + 512]),
                                waits=[y_evac[cq]], sig=True)
                            if cq == NQC - 1:
                                ev_ylast = evsc
                        ev_yw = p.dma("DS", y_scr[pr, :, :], y_stage[:],
                                      waits=[ev_ylast])
                        y_stage_free = ev_yw
                        if pr == 0:
                            y_written = [None] * NPAIR
                        y_written[pr] = ev_yw
                    bank_gate = p.join_cv()
                    if debug_stage == "pairs" and b == 0:
                        dbg_dump("qT", qT[:], p.join_cv())
                        dbg_dump("kT", kT[:], p.join_cv())
                        dbg_dump("krT", krT[:], p.join_cv())
                        dbg_dump("v", v_sb[:], p.join_cv())
                        dbg_dump("denacc", den_acc[:], p.join_cv())
                        dbg_dump("densum", den_sum[:], p.join_cv())
                        dbg_dump("deninvT", den_invT[:], p.join_cv())
                        dbg_dump("denscr", den_scr, [y_written[NPAIR - 1]])
                        dbg_dump("yscr", y_scr, [y_written[NPAIR - 1]])
                        dbg_finish()
                        return nc
                    p.flush(block)

                with nc.psum_tensor(_nm("wout_ps"), [128, 4 * 1024], F32) as wout_ps:
                    yst_free = [None, None]
                    wout_ps_free = None
                    x_f32_free = None
                    for g in range(NT // 4):
                        ev_group = None
                        for pr in range(NPAIR):
                            ysl = pr % 2
                            evl = p.dma("DA", yst_sb[:, ysl * 512:(ysl + 1) * 512],
                                        y_scr[pr, :, g * 512:(g + 1) * 512],
                                        waits=[y_written[pr], yst_free[ysl]])
                            evm = None
                            for tsub in range(4):
                                for nh in range(2):
                                    w3 = [evl, ev_woutcast] + bank_gate
                                    if pr == 0 and tsub == 0 and nh == 0:
                                        w3.append(wout_ps_free)
                                    evm = p.op("T", lambda eng, pr=pr, tsub=tsub, nh=nh, ysl=ysl: eng.matmul(
                                        wout_ps[:, tsub * 1024 + nh * 512: tsub * 1024 + (nh + 1) * 512],
                                        yst_sb[:, ysl * 512 + tsub * 128: ysl * 512 + (tsub + 1) * 128],
                                        wout_sb[:, pr * D + nh * 512: pr * D + (nh + 1) * 512],
                                        start=(pr == 0), stop=(pr == NPAIR - 1),
                                        skip_group_check=True),
                                        waits=w3 if (tsub == 0 and nh == 0) else (),
                                        sig=(tsub == 3 and nh == 1))
                            yst_free[ysl] = evm
                            ev_group = evm
                        ep_last = None
                        for tsub in range(4):
                            tt = g * 4 + tsub
                            evx = p.dma("DS", x_f32[:],
                                        xin[b, tt * 128:(tt + 1) * 128, :],
                                        waits=[x_f32_free])
                            p.op("V", lambda eng, tsub=tsub: eng.tensor_add(
                                t_f32[:], wout_ps[:, tsub * 1024:(tsub + 1) * 1024],
                                x_f32[:]),
                                waits=[ev_group, evx])
                            ev1 = p.op("V", lambda eng: eng.tensor_add(
                                t_f32[:], t_f32[:], bout_bc[:]), sig=True)
                            x_f32_free = ev1
                            _f1 = ev1
                            if debug_stage == "woutpre":
                                evw = p.dma("DS",
                                            x1_scr[b * S + tt * 128: b * S + (tt + 1) * 128, :],
                                            t_f32[:], waits=[ev1])
                                ep_last = evw
                                x1_done.append(evw)
                                ev7 = evw
                                continue
                            evrd = p.op("V", lambda eng: eng.reduce_sum(
                                stat[:, 0:1], t_f32[:], axis=AX.X), force=[_f1],
                                sig=True)
                            evmu = p.op("V", lambda eng: eng.tensor_scalar_mul(
                                stat[:, 1:2], stat[:, 0:1], 1.0 / D), force=[evrd],
                                sig=True)
                            ev2 = p.op("V", lambda eng: eng.tensor_scalar(
                                t_f32[:], t_f32[:], stat[:, 1:2], None,
                                ALU.subtract), force=[evmu], sig=True)
                            ev3 = p.op("S", lambda eng: eng.activation(
                                trash_b[:, 0:D], t_f32[:], AF.Square,
                                accum_out=stat[:, 2:3]), waits=[ev2], sig=True)
                            ev4 = ev3
                            if debug_stage == "wout" and b == 0:
                                p.op("V", lambda eng, tt=tt: eng.tensor_copy(
                                    lndbg[:, tt:tt + 1], stat[:, 2:3]), waits=[ev3])
                            ev5 = p.op("S", lambda eng: eng.activation(
                                stat[:, 4:5], stat[:, 2:3], AF.Ln,
                                scale=1.0 / D, bias=eps_sb[:, 0:1]),
                                waits=[ev_eps], force=[ev3], sig=True)
                            ev6 = p.op("S", lambda eng: eng.activation(
                                stat[:, 5:6], stat[:, 4:5], AF.Exp, scale=-0.5),
                                sig=True)
                            if debug_stage == "wout" and b == 0:
                                p.op("V", lambda eng, tt=tt: eng.tensor_copy(
                                    lndbg[:, 64 + tt:64 + tt + 1], stat[:, 5:6]),
                                    waits=[ev6])
                            ev7 = p.op("V", lambda eng: eng.tensor_scalar(
                                t_f32[:], t_f32[:], stat[:, 5:6], None, ALU.mult),
                                waits=[ev6], sig=True)
                            evw = p.dma("DS",
                                        x1_scr[b * S + tt * 128: b * S + (tt + 1) * 128, :],
                                        t_f32[:], waits=[ev7])
                            ep_last = evw
                            x1_done.append(evw)
                        wout_ps_free = ev7
                    bank_gate = p.join_cv()
                    if debug_stage in ("wout", "woutpre") and b == 0:
                        dbg_dump("x1scr", x1_scr, [x1_done[-1]])
                        if debug_stage == "wout":
                            dbg_dump("lndbg", lndbg[:], p.join_cv())
                        dbg_finish()
                        return nc
                    p.flush(block)
            ev_x1_all = list(x1_done)
            p1_gate = p.join_cv()

        with ExitStack() as c2:
            sb2 = lambda n, sh, d_: c2.enter_context(nc.sbuf_tensor(n, sh, d_))
            x1t = sb2("x1t", [128, 2 * D], F32)
            tt_f = sb2("tt_f", [128, D], F32)
            trash2 = sb2("trash2", [128, D], F32)
            stat2 = sb2("stat2", [128, 8], F32)
            xfT_f = sb2("xfT_f", [128, 8 * 128], F32)
            xfn_b4 = sb2("xfn_b4", [128, 8 * 512], BF16)
            logit_sb = sb2("logit_sb", [128, 8], F32)
            eq_sb = sb2("eq_sb", [128, 8], F32)
            we8 = sb2("we8", [128, 8], F32)
            weT_all = sb2("weT_all", [8, TOK], F32)

            with ExitStack() as cp2:
                trf_ps = cp2.enter_context(nc.psum_tensor(_nm("trf_ps"), [128, 2 * 512], F32))
                log_ps = cp2.enter_context(nc.psum_tensor(_nm("log_ps"), [128, 8], F32))
                tr8_ps = cp2.enter_context(nc.psum_tensor(_nm("tr8_ps"), [128, 128], F32))

                x1t_free = [None, None]
                trf_free = [None, None]
                log_free = None
                tr8_free = None
                stage_ev = None
                for tt in range(NTT):
                    sl = tt % 2
                    evx = p.dma("DS", x1t[:, sl * D:(sl + 1) * D],
                                x1_scr[tt * 128:(tt + 1) * 128, :],
                                waits=[x1t_free[sl]] + (ev_x1_all[-1:] if tt == 0 else []))
                    evrd = p.op("V", lambda eng, sl=sl: eng.reduce_sum(
                        stat2[:, 0:1], x1t[:, sl * D:(sl + 1) * D], axis=AX.X),
                        waits=[evx] + p1_gate, sig=True)
                    evmu = p.op("V", lambda eng: eng.tensor_scalar_mul(
                        stat2[:, 1:2], stat2[:, 0:1], 1.0 / D), force=[evrd],
                        sig=True)
                    ev2 = p.op("V", lambda eng, sl=sl: eng.tensor_scalar(
                        tt_f[:], x1t[:, sl * D:(sl + 1) * D], stat2[:, 1:2], None,
                        ALU.subtract), force=[evmu], sig=True)
                    x1t_free[sl] = ev2
                    ev3 = p.op("S", lambda eng: eng.activation(
                        trash2[:], tt_f[:], AF.Square, accum_out=stat2[:, 2:3]),
                        waits=[ev2] + p1_gate, sig=True)
                    ev4 = ev3
                    ev5 = p.op("S", lambda eng: eng.activation(
                        stat2[:, 4:5], stat2[:, 2:3], AF.Ln,
                        scale=1.0 / D, bias=eps_sb[:, 0:1]),
                        waits=[ev_eps], force=[ev3], sig=True)
                    ev6 = p.op("S", lambda eng: eng.activation(
                        stat2[:, 5:6], stat2[:, 4:5], AF.Exp, scale=-0.5), sig=True)
                    p.op("V", lambda eng: eng.tensor_scalar(
                        tt_f[:], tt_f[:], stat2[:, 5:6], None, ALU.mult),
                        waits=[ev6])
                    p.op("V", lambda eng: eng.tensor_mul(tt_f[:], tt_f[:], lnw_bc[:]))
                    ev7 = p.op("V", lambda eng: eng.tensor_add(
                        tt_f[:], tt_f[:], lnb_bc[:]), sig=True)
                    evlog = None
                    for c in range(8):
                        ts = c % 2
                        evt = p.op("T", lambda eng, c=c, ts=ts: eng.transpose(
                            trf_ps[:, ts * 512: ts * 512 + 128],
                            tt_f[:, c * 128:(c + 1) * 128], identf[:]),
                            waits=[ev7, trf_free[ts]] + (p1_gate if tt == 0 else []),
                            sig=True)
                        evcp = p.op("V", lambda eng, c=c, ts=ts: eng.tensor_copy(
                            xfT_f[:, c * 128:(c + 1) * 128],
                            trf_ps[:, ts * 512: ts * 512 + 128]),
                            waits=[evt], sig=True)
                        evb = p.op("V", lambda eng, c=c, tt=tt: eng.tensor_copy(
                            xfn_b4[:, c * 512 + (tt % 4) * 128: c * 512 + (tt % 4 + 1) * 128],
                            xfT_f[:, c * 128:(c + 1) * 128]),
                            waits=[stage_ev if tt % 4 == 0 else None], sig=True)
                        trf_free[ts] = evb
                        evlog = p.op("T", lambda eng, c=c: eng.matmul(
                            log_ps[:], xfT_f[:, c * 128:(c + 1) * 128],
                            routsb[:, c * 8:(c + 1) * 8],
                            start=(c == 0), stop=(c == 7)),
                            waits=[evcp] + ([log_free] if c == 0 else []),
                            sig=(c == 7))
                    if tt % 4 == 3:
                        evst = [p.dma("DS",
                                      xfn_scr[c, :, (tt - 3) * 128:(tt + 1) * 128],
                                      xfn_b4[:, c * 512:(c + 1) * 512],
                                      waits=[evb]) for c in range(8)]
                        stage_ev = evst[-1]
                    evlc = p.op("V", lambda eng: eng.tensor_copy(
                        logit_sb[:], log_ps[:]), waits=[evlog], sig=True)
                    log_free = evlc
                    evrm = p.op("V", lambda eng: eng.reduce_max(
                        stat2[:, 6:7], logit_sb[:], axis=AX.X), force=[evlc],
                        sig=True)
                    ev8 = p.op("V", lambda eng: eng.tensor_scalar(
                        eq_sb[:], logit_sb[:], stat2[:, 6:7], None, ALU.is_equal),
                        force=[evrm], sig=True)
                    ev9 = p.op("S", lambda eng: eng.activation(
                        stat2[:, 7:8], stat2[:, 6:7], AF.Exp, scale=-1.0),
                        waits=[ev8], sig=True)
                    ev9b = p.op("V", lambda eng: eng.tensor_scalar_add(
                        stat2[:, 7:8], stat2[:, 7:8], 1.0), waits=[ev9], sig=True)
                    ev10 = p.op("V", lambda eng: eng.reciprocal(
                        stat2[:, 7:8], stat2[:, 7:8]), force=[ev9b], sig=True)
                    ev11 = p.op("V", lambda eng: eng.tensor_scalar(
                        we8[:], eq_sb[:], stat2[:, 7:8], None, ALU.mult),
                        force=[ev10], sig=True)
                    evt8 = p.op("T", lambda eng: eng.transpose(
                        tr8_ps[0:8, :], we8[:], identf[:]),
                        waits=[ev11, tr8_free], sig=True)
                    tr8_free = p.op("V", lambda eng, tt=tt: eng.tensor_copy(
                        weT_all[:, tt * 128:(tt + 1) * 128], tr8_ps[0:8, 0:128]),
                        waits=[evt8], sig=True)
                ev_wescr = p.dma("DS", we_scr, weT_all[:], waits=[tr8_free])
                if debug_stage == "p2":
                    dbg_dump("weT", weT_all[:], p.join_cv())
                    dbg_dump("logit", logit_sb[:], p.join_cv())
                    dbg_dump("eq", eq_sb[:], p.join_cv())
                    dbg_dump("we8", we8[:], p.join_cv())
                    dbg_dump("stat2", stat2[:], p.join_cv())
                    dbg_dump("wescr", we_scr, [ev_wescr])
                    dbg_finish()
                    return nc
                p2_gate = p.join_cv()
                p.flush(block)

        with ExitStack() as c3:
            sb3 = lambda n, sh, d_: c3.enter_context(nc.sbuf_tensor(n, sh, d_))
            xfnT8 = sb3("xfnT8", [128, 8 * ETOK], BF16)
            xfeT8 = sb3("xfeT8", [128, 8 * ETOK], BF16)
            hmidT = sb3("hmidT", [128, 16 * ETOK], BF16)
            routed8 = sb3("routed8", [128, ETT * D], BF16)
            g_buf = sb3("g_buf", [128, 2 * 8 * 512], BF16)
            u_buf = sb3("u_buf", [128, 2 * 8 * 512], BF16)
            d_buf = sb3("d_buf", [128, 16 * 1024], BF16)
            bc3_sb = sb3("bc3_sb", [128, ETOK], F32)
            silu_sb = sb3("silu_sb", [128, 2 * 512], BF16)
            we_rows8 = sb3("we_rows8", [1, 8 * ETOK], F32)

            with ExitStack() as cp3:
                hg_ps = cp3.enter_context(nc.psum_tensor(_nm("hg_ps"), [128, 2 * 512], F32))
                hu_ps = cp3.enter_context(nc.psum_tensor(_nm("hu_ps"), [128, 2 * 512], F32))
                rt_ps = cp3.enter_context(nc.psum_tensor(_nm("rt_ps"), [128, 1024], F32))
                bc3_ps = cp3.enter_context(nc.psum_tensor(_nm("bc3_ps"), [128, 2 * 512], F32))

                xfn8_ev = None
                gu_used = {}
                d_used = None
                d_ev = None
                hg_free = [None, None]
                hu_free = [None, None]
                rt_free = None
                bc3_free = [None, None]
                silu_free = [None, None]
                hmid_used = None
                gateup_done = None
                routed_wr = None

                NMG = 4
                we8_lastuse = None
                for q in range(NEI):
                    evx8 = [p.dma("DS", xfnT8[:, c * ETOK:(c + 1) * ETOK],
                                  xfn_scr[c, :, q * ETOK:(q + 1) * ETOK],
                                  waits=[stage_ev, gateup_done])
                            for c in range(8)]
                    xfn8_ev = evx8[-1]
                    ev_we8 = p.dma(
                        "DS", we_rows8[0:1, :].rearrange("o (e n) -> o e n", e=8),
                        we_scr[:, q * ETOK:(q + 1) * ETOK],
                        waits=[ev_wescr, we8_lastuse])
                    for e in range(NE + 1):
                        wd = p2_gate + ([] if d_used is None else [d_used])
                        d_ev = p.dma("DG", d_buf[:].rearrange("p (m n) -> p m n", n=1024),
                                     down_in[e].rearrange("(m p) n -> p m n", p=128),
                                     waits=wd)
                        if e < NE:
                            evbc = []
                            for cc in range(ECH):
                                csl = cc % 2
                                evb1 = p.op("T", lambda eng, e=e, cc=cc, csl=csl: eng.matmul(
                                    bc3_ps[:, csl * 512:(csl + 1) * 512],
                                    ones_f[0:1, :],
                                    we_rows8[0:1, e * ETOK + cc * 512: e * ETOK + (cc + 1) * 512],
                                    start=True, stop=True),
                                    waits=[ev_we8, bc3_free[csl]] + (p2_gate if q == 0 and e == 0 else []),
                                    sig=True)
                                we8_lastuse = evb1
                                evb2 = p.op("V", lambda eng, cc=cc, csl=csl: eng.tensor_copy(
                                    bc3_sb[:, cc * 512:(cc + 1) * 512],
                                    bc3_ps[:, csl * 512:(csl + 1) * 512]),
                                    waits=[evb1], sig=True)
                                bc3_free[csl] = evb2
                                evbc.append(evb2)
                            xfe_ev = None
                            for c in range(8):
                                xfe_ev = p.op("V", lambda eng, c=c: eng.tensor_mul(
                                    xfeT8[:, c * ETOK:(c + 1) * ETOK],
                                    xfnT8[:, c * ETOK:(c + 1) * ETOK],
                                    bc3_sb[:]),
                                    waits=[evbc[-1], xfn8_ev, gateup_done], sig=True)
                            src8 = xfeT8
                        else:
                            xfe_ev = xfn8_ev
                            src8 = xfnT8
                        hm_ev = None
                        for mg in range(NMG):
                            gsl = (e * NMG + mg) % 2
                            wg = p2_gate + [gu_used.get(("g", gsl))]
                            evg = p.dma("DG", g_buf[:, gsl * 4096:(gsl + 1) * 4096].rearrange(
                                "p (k n) -> p k n", n=512),
                                gate_in[e].rearrange("(k p) i -> p k i", p=128)[
                                    :, :, mg * 512:(mg + 1) * 512],
                                waits=wg)
                            evu = p.dma("DG", u_buf[:, gsl * 4096:(gsl + 1) * 4096].rearrange(
                                "p (k n) -> p k n", n=512),
                                up_in[e].rearrange("(k p) i -> p k i", p=128)[
                                    :, :, mg * 512:(mg + 1) * 512],
                                waits=p2_gate + [gu_used.get(("u", gsl))])
                            for mm in range(4):
                                m = mg * 4 + mm
                                for cc in range(ECH):
                                    hsl = (m * ECH + cc) % 2
                                    evh = evu2 = None
                                    for k in range(8):
                                        evh = p.op("T", lambda eng, k=k, m=m, mm=mm, cc=cc, hsl=hsl, gsl=gsl, src8=src8: eng.matmul(
                                            hg_ps[:, hsl * 512:(hsl + 1) * 512],
                                            g_buf[:, gsl * 4096 + k * 512 + mm * 128: gsl * 4096 + k * 512 + (mm + 1) * 128],
                                            src8[:, k * ETOK + cc * 512: k * ETOK + cc * 512 + 512],
                                            start=(k == 0), stop=(k == 7)),
                                            waits=[evg, xfe_ev, hg_free[hsl]] if k == 0 else (),
                                            sig=(k == 7))
                                    for k in range(8):
                                        evu2 = p.op("T", lambda eng, k=k, m=m, mm=mm, cc=cc, hsl=hsl, gsl=gsl, src8=src8: eng.matmul(
                                            hu_ps[:, hsl * 512:(hsl + 1) * 512],
                                            u_buf[:, gsl * 4096 + k * 512 + mm * 128: gsl * 4096 + k * 512 + (mm + 1) * 128],
                                            src8[:, k * ETOK + cc * 512: k * ETOK + cc * 512 + 512],
                                            start=(k == 0), stop=(k == 7)),
                                            waits=[evu, xfe_ev, hu_free[hsl]] if k == 0 else (),
                                            sig=(k == 7))
                                    evsl = p.op("S", lambda eng, hsl=hsl: eng.activation(
                                        silu_sb[:, hsl * 512:(hsl + 1) * 512],
                                        hg_ps[:, hsl * 512:(hsl + 1) * 512],
                                        AF.Silu),
                                        waits=[evh, silu_free[hsl]], sig=True)
                                    hg_free[hsl] = evsl
                                    evhm = p.op("V", lambda eng, m=m, cc=cc, hsl=hsl: eng.tensor_mul(
                                        hmidT[:, m * ETOK + cc * 512: m * ETOK + cc * 512 + 512],
                                        silu_sb[:, hsl * 512:(hsl + 1) * 512],
                                        hu_ps[:, hsl * 512:(hsl + 1) * 512]),
                                        waits=[evsl, evu2, hmid_used], sig=True)
                                    hu_free[hsl] = evhm
                                    silu_free[hsl] = evhm
                                    hm_ev = evhm
                            gu_used[("g", gsl)] = evh
                            gu_used[("u", gsl)] = evu2
                        gateup_done = hm_ev
                        dn_last = None
                        for tt in range(ETT):
                            evd = None
                            for m in range(16):
                                for nh in range(2):
                                    evd = p.op("T", lambda eng, m=m, tt=tt, nh=nh: eng.matmul(
                                        rt_ps[:, nh * 512:(nh + 1) * 512],
                                        hmidT[:, m * ETOK + tt * 128: m * ETOK + (tt + 1) * 128],
                                        d_buf[:, m * 1024 + nh * 512: m * 1024 + (nh + 1) * 512],
                                        start=(m == 0), stop=(m == 15),
                                        skip_group_check=True),
                                        waits=[hm_ev, d_ev, rt_free] if (m == 0 and nh == 0) else (),
                                        sig=(m == 15 and nh == 1))
                            if e == 0:
                                evrt = p.op("V", lambda eng, tt=tt: eng.tensor_copy(
                                    routed8[:, tt * D:(tt + 1) * D], rt_ps[:]),
                                    waits=[evd, routed_wr], sig=True)
                            else:
                                evrt = p.op("V", lambda eng, tt=tt: eng.tensor_add(
                                    routed8[:, tt * D:(tt + 1) * D],
                                    routed8[:, tt * D:(tt + 1) * D], rt_ps[:]),
                                    waits=[evd], sig=True)
                            rt_free = evrt
                            dn_last = evrt
                        d_used = evd
                        hmid_used = evd
                    routed_wr = p.dma(
                        "DS",
                        routed_scr[q * ETOK:(q + 1) * ETOK, :].rearrange(
                            "(t p) d -> p t d", p=128),
                        routed8[:].rearrange("p (t d) -> p t d", d=D),
                        waits=[dn_last])
                p3_gate = p.join_cv()
                ev_routed_all = routed_wr
                if debug_stage == "p3":
                    dbg_dump("x1scr", x1_scr, [ev_x1_all[-1]])
                    dbg_dump("xfnscr", xfn_scr, [stage_ev])
                    dbg_dump("wescr", we_scr, [ev_wescr])
                    dbg_dump("routedscr", routed_scr, [routed_wr])
                    dbg_finish()
                    return nc
                p.flush(block)

        with ExitStack() as c4:
            sb4 = lambda n, sh, d_: c4.enter_context(nc.sbuf_tensor(n, sh, d_))
            x1f = sb4("x1f", [128, 2 * D], F32)
            rtf = sb4("rtf", [128, 2 * D], BF16)
            tf4 = sb4("tf4", [128, D], F32)
            trash4 = sb4("trash4", [128, D], F32)
            stat4 = sb4("stat4", [128, 8], F32)

            x1f_free = [None, None]
            rtf_free = [None, None]
            for tt in range(NTT):
                sl = tt % 2
                b = tt // NT
                r0 = (tt % NT) * 128
                evx = p.dma("DS", x1f[:, sl * D:(sl + 1) * D],
                            x1_scr[tt * 128:(tt + 1) * 128, :],
                            waits=[x1f_free[sl]] + ([ev_routed_all] if tt == 0 else []))
                evr = p.dma("DA", rtf[:, sl * D:(sl + 1) * D],
                            routed_scr[tt * 128:(tt + 1) * 128, :],
                            waits=[rtf_free[sl]] + ([ev_routed_all] if tt == 0 else []))
                ev1 = p.op("V", lambda eng, sl=sl: eng.scalar_tensor_tensor(
                    tf4[:], x1f[:, sl * D:(sl + 1) * D], 2.0,
                    rtf[:, sl * D:(sl + 1) * D], ALU.mult, ALU.add),
                    waits=[evx, evr] + (p3_gate if tt == 0 else []), sig=True)
                x1f_free[sl] = ev1
                rtf_free[sl] = ev1
                evrd = p.op("V", lambda eng: eng.reduce_sum(
                    stat4[:, 0:1], tf4[:], axis=AX.X), force=[ev1], sig=True)
                evmu = p.op("V", lambda eng: eng.tensor_scalar_mul(
                    stat4[:, 1:2], stat4[:, 0:1], 1.0 / D), force=[evrd],
                    sig=True)
                ev2 = p.op("V", lambda eng: eng.tensor_scalar(
                    tf4[:], tf4[:], stat4[:, 1:2], None, ALU.subtract),
                    force=[evmu], sig=True)
                ev3 = p.op("S", lambda eng: eng.activation(
                    trash4[:], tf4[:], AF.Square, accum_out=stat4[:, 2:3]),
                    waits=[ev2] + (p3_gate if tt == 0 else []), sig=True)
                ev4 = ev3
                ev5 = p.op("S", lambda eng: eng.activation(
                    stat4[:, 4:5], stat4[:, 2:3], AF.Ln,
                    scale=1.0 / D, bias=eps_sb[:, 0:1]),
                    waits=[ev_eps], force=[ev3], sig=True)
                ev6 = p.op("S", lambda eng: eng.activation(
                    stat4[:, 5:6], stat4[:, 4:5], AF.Exp, scale=-0.5),
                    force=[ev5], sig=True)
                ev7 = p.op("V", lambda eng: eng.tensor_scalar(
                    tf4[:], tf4[:], stat4[:, 5:6], None, ALU.mult),
                    waits=[ev6], sig=True)
                p.dma("DS", out[b, r0:r0 + 128, :], tf4[:], waits=[ev7])
            p.op("SY", lambda eng: eng.nop(),
                 waits=[(ch, p.count[ch]) for ch in DMA_CH if p.count[ch] > 0])
            p.flush(block)
    return nc


def _pack_inputs(ins, S=4096):
    f32 = np.float32
    bf16 = ml_dtypes.bfloat16
    x = np.ascontiguousarray(np.asarray(ins["L0_attn_x"], f32))
    w_qkv = np.asarray(ins["L0_attn_w_qkv"], f32)
    wqkvT = np.ascontiguousarray(w_qkv.T)
    wpk = np.empty((NPAIR, 128, 8 * 384), f32)
    for pr in range(NPAIR):
        t = np.concatenate([wqkvT[:, 128 * pr:128 * pr + 128],
                            wqkvT[:, D + 128 * pr:D + 128 * pr + 128],
                            wqkvT[:, 2 * D + 128 * pr:2 * D + 128 * pr + 128]],
                           axis=1)
        wpk[pr] = t.reshape(8, 128, 384).transpose(1, 0, 2).reshape(128, 8 * 384)
    w_out = np.asarray(ins["L0_attn_w_out"], f32)
    woutpk = np.ascontiguousarray(
        w_out.T.reshape(8, 128, D).transpose(1, 0, 2).reshape(128, 8 * D))
    router = np.asarray(ins["L0_ffn_router_w"], f32)
    routpk = np.ascontiguousarray(
        router.reshape(8, 128, 8).transpose(1, 0, 2).reshape(128, 64))
    rows = np.stack([np.asarray(ins["L0_attn_b_out"], f32),
                     np.asarray(ins["L0_ffn_ln_w"], f32),
                     np.asarray(ins["L0_ffn_ln_b"], f32)], axis=0)
    i_idx = (np.arange(128) % 64) // 2
    theta = 10000.0 ** (-2.0 * (i_idx - 1.0) / 64.0)
    tpos = np.arange(S, dtype=np.float64)
    ang = tpos[None, :] * theta[:, None]
    ctab = np.cos(ang).astype(bf16)
    sgn = np.where(np.arange(128) % 2 == 0, 1.0, -1.0)[:, None]
    stab = (np.sin(ang) * sgn).astype(bf16)
    pgrid = np.arange(128)[:, None]
    ngrid = np.arange(512)[None, :]
    masks = np.concatenate(
        [np.where(ngrid >= 128 * tq + pgrid, 0.0, NEG) for tq in range(4)],
        axis=1).astype(bf16)
    identb = np.eye(128, dtype=bf16)
    identf = np.eye(128, dtype=f32)
    gate = np.concatenate([np.asarray(ins["L0_ffn_gate_w"], f32),
                           np.asarray(ins["L0_ffn_sh_gate"], f32)[None]], axis=0)
    up = np.concatenate([np.asarray(ins["L0_ffn_up_w"], f32),
                         np.asarray(ins["L0_ffn_sh_up"], f32)[None]], axis=0)
    down = np.concatenate([np.asarray(ins["L0_ffn_down_w"], f32),
                           np.asarray(ins["L0_ffn_sh_down"], f32)[None]], axis=0)
    shared = {"wpk_in": wpk, "woutpk": woutpk, "routpk": routpk, "rows_in": rows,
              "ctab_in": np.ascontiguousarray(ctab),
              "stab_in": np.ascontiguousarray(stab),
              "masks_in": np.ascontiguousarray(masks),
              "identb_in": identb, "identf_in": identf,
              "gate_in": np.ascontiguousarray(gate),
              "up_in": np.ascontiguousarray(up),
              "down_in": np.ascontiguousarray(down)}
    n_b = x.shape[0]
    in_maps = []
    for c in range(n_b // B_CORE):
        m = dict(shared)
        m["xin"] = np.ascontiguousarray(x[c * B_CORE:(c + 1) * B_CORE])
        in_maps.append(m)
    return in_maps


def kernel(**inputs):
    S = int(np.asarray(inputs["L0_attn_x"]).shape[1])
    in_maps = _pack_inputs(inputs, S=S)
    nc = build_core_program(S=S)
    res = run_bass_kernel_spmd(nc, in_maps, core_ids=list(range(len(in_maps))))
    outs = [np.asarray(r["out"], dtype=np.float32) for r in res.results]
    return np.concatenate(outs, axis=0)


# version 04
# speedup vs baseline: 1.0028x; 1.0028x over previous
import numpy as np
import ml_dtypes
import concourse.bass as bass
import concourse.mybir as mybir
from concourse.bass_utils import run_bass_kernel_spmd
from contextlib import ExitStack

F32 = mybir.dt.float32
BF16 = mybir.dt.bfloat16
AF = mybir.ActivationFunctionType
ALU = mybir.AluOpType
AX = mybir.AxisListType

D = 1024
H = 16
DH = 64
NPAIR = 8
NE = 8
I_FF = 2048
EPS = 1e-5
B_CORE = 2
NCORES = 8
NEG = -1.0e30

DMA_CH = ("DS", "DG", "DA")
CH_ENGINE = {"T": "tensor", "V": "vector", "S": "scalar", "G": "gpsimd",
             "SY": "sync", "DS": "sync", "DG": "gpsimd", "DA": "scalar"}


class Prog:

    def __init__(self, nc, es):
        self.nc = nc
        self.ops = []
        self.count = {ch: 0 for ch in CH_ENGINE}
        self.sems = {ch: es.enter_context(nc.semaphore("sem_" + ch))
                     for ch in CH_ENGINE if ch != "SY"}
        self.seen = {}

    def op(self, ch, fn, waits=(), sig=False, force=()):
        waits = [w for w in waits if w is not None]
        force = [w for w in force if w is not None]
        for (wc, wv) in waits + force:
            assert wv <= self.count[wc], (wc, wv, self.count[wc])
        inc = 16 if ch in DMA_CH else (1 if sig else 0)
        self.ops.append((ch, fn, tuple(waits), inc, tuple(force)))
        if inc:
            self.count[ch] += inc
            return (ch, self.count[ch])
        return None

    def dma(self, ch, out, in_, waits=()):
        assert ch in DMA_CH
        return self.op(ch, lambda eng, o=out, i=in_: eng.dma_start(out=o, in_=i),
                       waits=waits)

    def join_cv(self):
        return [(c, self.count[c]) for c in ("T", "V", "S") if self.count[c] > 0]

    def flush(self, block):
        by_eng = {}
        for rec in self.ops:
            by_eng.setdefault(CH_ENGINE[rec[0]], []).append(rec)
        self.ops = []
        for eng_name in ("sync", "gpsimd", "tensor", "vector", "scalar"):
            recs = by_eng.get(eng_name)
            if not recs:
                continue
            seen = self.seen.setdefault(eng_name, {c: 0 for c in CH_ENGINE})

            def emit(eng, recs=recs, seen=seen, eng_name=eng_name):
                for ch, fn, waits, inc, force in recs:
                    for wc, wv in waits:
                        if wc not in DMA_CH and CH_ENGINE[wc] == eng_name:
                            continue
                        if wv <= seen[wc]:
                            continue
                        eng.wait_ge(self.sems[wc], wv)
                        seen[wc] = wv
                    for wc, wv in force:
                        eng.wait_ge(self.sems[wc], wv)
                    inst = fn(eng)
                    if inc:
                        inst.then_inc(self.sems[ch], inc)

            getattr(block, eng_name)(emit)


def build_core_program(S=4096, debug=False, debug_stage=None):
    NT = S // 128
    NQC = S // 512
    NKT = S // 128
    TOK = B_CORE * S
    NTT = TOK // 128
    DEN_W = min(1024, S)
    NPD = S // DEN_W
    ETOK = min(1024, TOK)
    NEI = TOK // ETOK
    ETT = ETOK // 128
    ECH = ETOK // 512
    assert NT % 4 == 0

    nc = bass.Bass("TRN2", target_bir_lowering=False)
    dt = nc.dram_tensor
    _uid = [0]

    def _nm(s):
        _uid[0] += 1
        return f"{s}_{_uid[0]}"

    xin = dt("xin", [B_CORE, S, D], F32, kind="ExternalInput").ap()
    wpk_in = dt("wpk_in", [NPAIR, 128, 8 * 384], F32, kind="ExternalInput").ap()
    woutpk = dt("woutpk", [128, 8 * D], F32, kind="ExternalInput").ap()
    routpk = dt("routpk", [128, 8 * 8], F32, kind="ExternalInput").ap()
    rows_in = dt("rows_in", [3, D], F32, kind="ExternalInput").ap()
    ctab_in = dt("ctab_in", [128, S], BF16, kind="ExternalInput").ap()
    stab_in = dt("stab_in", [128, S], BF16, kind="ExternalInput").ap()
    masks_in = dt("masks_in", [128, 4 * 512], BF16, kind="ExternalInput").ap()
    identb_in = dt("identb_in", [128, 128], BF16, kind="ExternalInput").ap()
    identf_in = dt("identf_in", [128, 128], F32, kind="ExternalInput").ap()
    gate_in = dt("gate_in", [NE + 1, D, I_FF], F32, kind="ExternalInput").ap()
    up_in = dt("up_in", [NE + 1, D, I_FF], F32, kind="ExternalInput").ap()
    down_in = dt("down_in", [NE + 1, I_FF, D], F32, kind="ExternalInput").ap()

    out = dt("out", [B_CORE, S, D], F32, kind="ExternalOutput").ap()

    y_scr = dt("y_scr", [NPAIR, 128, S], BF16, kind="Internal").ap()
    x1_scr = dt("x1_scr", [TOK, D], F32, kind="Internal").ap()
    xfn_scr = dt("xfn_scr", [8, 128, TOK], BF16, kind="Internal").ap()
    routed_scr = dt("routed_scr", [TOK, D], BF16, kind="Internal").ap()
    we_scr = dt("we_scr", [8, TOK], F32, kind="Internal").ap()
    den_scr = dt("den_scr", [2, S], F32, kind="Internal").ap()
    if debug:
        dbg_x1 = dt("dbg_x1", [TOK, D], F32, kind="ExternalOutput").ap()
        dbg_y = dt("dbg_y", [NPAIR, 128, S], BF16, kind="ExternalOutput").ap()
        dbg_xfn = dt("dbg_xfn", [8, 128, TOK], BF16, kind="ExternalOutput").ap()
        dbg_we = dt("dbg_we", [8, TOK], F32, kind="ExternalOutput").ap()
        dbg_rt = dt("dbg_rt", [TOK, D], BF16, kind="ExternalOutput").ap()
        dbg_den = dt("dbg_den", [2, S], F32, kind="ExternalOutput").ap()

    with ExitStack() as ctx:
        p = Prog(nc, ctx)
        sbuf = lambda n, sh, d_: ctx.enter_context(nc.sbuf_tensor(n, sh, d_))
        block = ctx.enter_context(nc.Block())
        dbg_bufs = {}

        def dbg_dump(name, src_ap, waits):
            shp = list(src_ap.shape)
            dtp = src_ap.dtype
            t = nc.dram_tensor("dump_" + name, shp, dtp, kind="ExternalOutput").ap()
            p.dma("DS", t, src_ap, waits=waits)

        def dbg_finish():
            p.op("SY", lambda eng: eng.nop(),
                 waits=[(ch, p.count[ch]) for ch in DMA_CH if p.count[ch] > 0])
            p.flush(block)

        identb = sbuf("identb", [128, 128], BF16)
        identf = sbuf("identf", [128, 128], F32)
        ones_f = sbuf("ones_f", [1, 128], F32)
        row_tmp = sbuf("row_tmp", [1, D], F32)
        bout_bc = sbuf("bout_bc", [128, D], F32)
        lnw_bc = sbuf("lnw_bc", [128, D], F32)
        lnb_bc = sbuf("lnb_bc", [128, D], F32)
        routsb = sbuf("routsb", [128, 8 * 8], F32)

        ld_const = [p.dma("DS", t[:], s) for t, s in
                    ((identb, identb_in), (identf, identf_in), (routsb, routpk))]
        ev_ones = p.op("V", lambda eng: eng.memset(ones_f[:], 1.0), sig=True)
        eps_sb = sbuf("eps_sb", [128, 1], F32)
        ev_eps = p.op("V", lambda eng: eng.memset(eps_sb[:], EPS), sig=True)

        with nc.psum_tensor(_nm("bcpre_ps"), [128, 512], F32) as bcpre_ps:
            prev = None
            for ri, dst in ((0, bout_bc), (1, lnw_bc), (2, lnb_bc)):
                evr = p.dma("DS", row_tmp[:], rows_in[ri:ri + 1, :],
                            waits=[prev])
                for c in range(D // 512):
                    evm = p.op("T", lambda eng, c=c: eng.matmul(
                        bcpre_ps[:], ones_f[0:1, :],
                        row_tmp[0:1, c * 512:(c + 1) * 512],
                        start=True, stop=True),
                        waits=[evr, ev_ones, prev], sig=True)
                    prev = p.op("V", lambda eng, c=c, dst=dst: eng.tensor_copy(
                        dst[:, c * 512:(c + 1) * 512], bcpre_ps[:]),
                        waits=[evm], sig=True)
            p.flush(block)

        with ExitStack() as c1:
            sb1 = lambda n, sh, d_: c1.enter_context(nc.sbuf_tensor(n, sh, d_))
            xT = sb1("xT", [128, 8 * S], BF16)
            xnat_b = sb1("xnat_b", [128, D], BF16)
            qT = sb1("qT", [128, S], BF16)
            kT = sb1("kT", [128, S], BF16)
            qr_ch = sb1("qr_ch", [128, 2 * 512], BF16)
            krT = sb1("krT", [128, S], BF16)
            qs_tmp = sb1("qs_tmp", [128, 512], BF16)
            rope_tmp = sb1("rope_tmp", [128, 512], BF16)
            v_sb = sb1("v_sb", [128, S], BF16)
            y_stage = sb1("y_stage", [128, 2 * S], BF16)
            exp_sb = sb1("exp_sb", [128, 2 * 1024], BF16)
            trash_b = sb1("trash_b", [128, max(DEN_W, D)], BF16)
            den_acc = sb1("den_acc", [128, NPD * 2 * NKT], F32)
            den_sum = sb1("den_sum", [128, 2 * NKT], F32)
            den_invT = sb1("den_invT", [64, 128], F32)
            den_row_ch = sb1("den_row_ch", [1, 2 * 512], F32)
            bc_sb = sb1("bc_sb", [128, 512], F32)
            wpk_b = sb1("wpk_b", [128, 2 * 3072], BF16)
            wout_sb = sb1("wout_sb", [128, 8 * D], BF16)
            yst_sb = sb1("yst_sb", [128, 2 * 512], BF16)
            x_f32 = sb1("x_f32", [128, D], F32)
            t_f32 = sb1("t_f32", [128, D], F32)
            stat = sb1("stat", [128, 8], F32)
            lndbg = sb1("lndbg", [128, 3 * 32], F32)
            ctab = sb1("ctab", [128, S], BF16)
            stab = sb1("stab", [128, S], BF16)
            masks = sb1("masks", [128, 4 * 512], BF16)

            ld_tab = [p.dma("DS", ctab[:], ctab_in),
                      p.dma("DS", stab[:], stab_in),
                      p.dma("DS", masks[:], masks_in)]
            ld_const = ld_const + ld_tab
            ev_woutcast = p.dma("DG", wout_sb[:], woutpk)

            x1_done = []
            bank_gate = []
            wpkb_free = [None, None]
            for b in range(B_CORE):
                with nc.psum_tensor(_nm("trb_ps"), [128, 2048], BF16) as trb_ps:
                    xnat_free = None
                    tr_free = [None, None]
                    ev_xT_last = None
                    for tt in range(NT):
                        sl = 0
                        evc = p.dma("DG", xnat_b[:],
                                    xin[b, tt * 128:(tt + 1) * 128, :],
                                    waits=[xnat_free] + ld_const)
                        for c in range(8):
                            ts = c % 2
                            evt = p.op("T", lambda eng, sl=sl, c=c, ts=ts: eng.transpose(
                                trb_ps[:, ts * 1024: ts * 1024 + 128],
                                xnat_b[:, sl * D + c * 128: sl * D + (c + 1) * 128],
                                identb[:]),
                                waits=[evc, tr_free[ts]] + bank_gate, sig=True)
                            eve = p.op("V", lambda eng, c=c, tt=tt, ts=ts: eng.tensor_copy(
                                xT[:, c * S + tt * 128: c * S + (tt + 1) * 128],
                                trb_ps[:, ts * 1024: ts * 1024 + 128]),
                                waits=[evt], sig=True)
                            tr_free[ts] = eve
                            if c == 7:
                                xnat_free = eve
                            ev_xT_last = eve
                    bank_gate = [ev_xT_last]
                    if debug_stage == "A1" and b == 0:
                        dbg_dump("xT", xT[:], p.join_cv())
                        dbg_finish()
                        return nc
                    p.flush(block)

                with ExitStack() as cps:
                    ser_ps = cps.enter_context(nc.psum_tensor(_nm("ser_ps"), [128, 512], F32))
                    den_ps = cps.enter_context(nc.psum_tensor(_nm("den_ps"), [128, 2 * DEN_W], F32))
                    num_ps = cps.enter_context(nc.psum_tensor(_nm("num_ps"), [128, 1024], F32))
                    yT_ps = cps.enter_context(nc.psum_tensor(_nm("yT_ps"), [128, 512], F32))

                    wpkf_free = None
                    y_stage_free = [None, None]
                    qr_slot_free = [None, None]
                    den_row_free = None
                    den_slot_ev = [None, None]
                    den_acc_free = None
                    exp_slot_ev = [None, None]
                    yT_free = None
                    num_free = None
                    ser_free = None
                    y_written = [None] * NPAIR
                    prev_post = None

                    for pr in range(NPAIR + 1):
                        if pr < NPAIR:
                            psl = pr % 2
                            ysl = pr % 2
                            evw = p.dma("DG", wpk_b[:, psl * 3072:(psl + 1) * 3072],
                                        wpk_in[pr, :, :], waits=[wpkb_free[psl]])
                            for colo, dstbuf in ((0, qT), (128, kT)):
                                for cc in range(NQC):
                                    ev = None
                                    for k in range(8):
                                        ev = p.op("T", lambda eng, k=k, cc=cc, colo=colo, psl=psl: eng.matmul(
                                            ser_ps[:],
                                            wpk_b[:, psl * 3072 + k * 384 + colo: psl * 3072 + k * 384 + colo + 128],
                                            xT[:, k * S + cc * 512: k * S + cc * 512 + 512],
                                            start=(k == 0), stop=(k == 7)),
                                            waits=([evw, ser_free, ev_xT_last] + bank_gate) if k == 0 else (),
                                            sig=(k == 7))
                                    eve = p.op("V", lambda eng, cc=cc, dstbuf=dstbuf: eng.tensor_copy(
                                        dstbuf[:, cc * 512:(cc + 1) * 512], ser_ps[:]),
                                        waits=[ev], sig=True)
                                    ser_free = eve
                                if colo == 0:
                                    q_ready = eve
                                else:
                                    k_ready = eve
                            for g in range(NKT // 4):
                                ev = None
                                first = True
                                for tsub in range(4):
                                    tt = g * 4 + tsub
                                    for k in range(8):
                                        ev = p.op("T", lambda eng, k=k, tt=tt, tsub=tsub, psl=psl: eng.matmul(
                                            ser_ps[:, tsub * 128:(tsub + 1) * 128],
                                            xT[:, k * S + tt * 128: k * S + (tt + 1) * 128],
                                            wpk_b[:, psl * 3072 + k * 384 + 256: psl * 3072 + k * 384 + 384],
                                            start=(tsub == 0 and k == 0), stop=(tsub == 3 and k == 7),
                                            skip_group_check=True),
                                            waits=[evw, ser_free] if first else (),
                                            sig=(tsub == 3 and k == 7))
                                        first = False
                                eve = p.op("V", lambda eng, g=g: eng.tensor_copy(
                                    v_sb[:, g * 512:(g + 1) * 512], ser_ps[:]),
                                    waits=[ev], sig=True)
                                ser_free = eve
                            v_ready = eve
                            wpkb_free[psl] = eve
                            ev_ropek = None
                            for cc in range(NQC):
                                evsh = p.op("V", lambda eng, cc=cc: eng.stream_shuffle(
                                    qs_tmp[:], kT[:, cc * 512:(cc + 1) * 512],
                                    [i ^ 1 for i in range(32)]), waits=[k_ready],
                                    force=[k_ready], sig=True)
                                p.op("V", lambda eng, cc=cc: eng.tensor_mul(
                                    krT[:, cc * 512:(cc + 1) * 512],
                                    ctab[:, cc * 512:(cc + 1) * 512],
                                    kT[:, cc * 512:(cc + 1) * 512]))
                                p.op("V", lambda eng, cc=cc: eng.tensor_mul(
                                    rope_tmp[:], stab[:, cc * 512:(cc + 1) * 512],
                                    qs_tmp[:]), force=[evsh])
                                ev_ropek = p.op("V", lambda eng, cc=cc: eng.tensor_add(
                                    krT[:, cc * 512:(cc + 1) * 512],
                                    krT[:, cc * 512:(cc + 1) * 512],
                                    rope_tmp[:]), sig=True)

                        if pr >= 1:
                            ppr = pr - 1
                            pysl, py_evac, pden = prev_post
                            if NPD == 1:
                                evp = p.op("V", lambda eng: eng.tensor_copy(
                                    den_sum[:], den_acc[:, 0:2 * NKT]),
                                    waits=pden, sig=True)
                            else:
                                evp = p.op("V", lambda eng: eng.tensor_add(
                                    den_sum[:], den_acc[:, 0:2 * NKT],
                                    den_acc[:, 2 * NKT:4 * NKT]),
                                    waits=pden, sig=True)
                                for pc in range(2, NPD):
                                    evp = p.op("V", lambda eng, pc=pc: eng.tensor_add(
                                        den_sum[:], den_sum[:],
                                        den_acc[:, pc * 2 * NKT:(pc + 1) * 2 * NKT]),
                                        sig=True)
                            den_acc_free = evp
                            evr = p.op("V", lambda eng: eng.reciprocal(
                                den_sum[:], den_sum[:]), waits=[evp], force=[evp],
                                sig=True)
                            evtr = p.op("T", lambda eng: eng.transpose(
                                num_ps[0:2 * NKT, 0:128], den_sum[:], identf[:]),
                                waits=[evr, num_free], sig=True)
                            evte = p.op("V", lambda eng: eng.tensor_copy(
                                den_invT[0:2 * NKT, :], num_ps[0:2 * NKT, 0:128]),
                                waits=[evtr], sig=True)
                            evrows = p.dma(
                                "DS",
                                den_scr.rearrange("h (t n) -> (h t) n", n=128),
                                den_invT[0:2 * NKT, :], waits=[evte])
                            evbe_last = None
                            for cq in range(NQC):
                                evdl = [p.dma("DS", den_row_ch[0:1, h * 512:(h + 1) * 512],
                                              den_scr[h, cq * 512:(cq + 1) * 512][None, :],
                                              waits=[evrows, den_row_free])
                                        for h in range(2)]
                                evbc = None
                                for h in range(2):
                                    evbc = p.op("T", lambda eng, h=h: eng.matmul(
                                        num_ps[64 * h: 64 * h + 64, 0:512],
                                        ones_f[0:1, 0:64],
                                        den_row_ch[0:1, h * 512:(h + 1) * 512],
                                        start=True, stop=True, skip_group_check=True),
                                        waits=[evdl[1], evte, evbe_last] if h == 0 else (),
                                        sig=(h == 1))
                                den_row_free = evbc
                                evbe = p.op("V", lambda eng: eng.tensor_copy(
                                    bc_sb[:, 0:512],
                                    num_ps[:, 0:512]), waits=[evbc], sig=True)
                                evbe_last = evbe
                                evsc = p.op("V", lambda eng, cq=cq, pysl=pysl: eng.tensor_mul(
                                    y_stage[:, pysl * S + cq * 512: pysl * S + (cq + 1) * 512],
                                    y_stage[:, pysl * S + cq * 512: pysl * S + (cq + 1) * 512],
                                    bc_sb[:, 0:512]),
                                    waits=[py_evac[cq]], sig=True)
                                if cq == NQC - 1:
                                    ev_ylast = evsc
                            num_free = evbe_last
                            ev_yw = p.dma("DS", y_scr[ppr, :, :],
                                          y_stage[:, pysl * S:(pysl + 1) * S],
                                          waits=[ev_ylast])
                            y_stage_free[pysl] = ev_yw
                            y_written[ppr] = ev_yw

                        if pr < NPAIR:
                            den_units = [(h, qt, pc) for h in range(2)
                                         for qt in range(NKT) for pc in range(NPD)]
                            num_units = [(cq, kt) for cq in range(NQC)
                                         for kt in range(4 * cq + 4)]
                            nd, nn = len(den_units), len(num_units)
                            di = 0
                            y_evac = {}
                            qr_ev = [None, None]
                            pend = None

                            def emit_av(pend_, ysl=ysl):
                                nonlocal yT_free
                                cq_, kt_, esl_, evex_, evnum_ = pend_
                                lastkt_ = 4 * cq_ + 3
                                evav = None
                                for h in range(2):
                                    evav = p.op("T", lambda eng, h=h, kt_=kt_, esl_=esl_, lastkt_=lastkt_: eng.matmul(
                                        yT_ps[64 * h:64 * h + 64, :],
                                        v_sb[:, kt_ * 128 + h * 64: kt_ * 128 + h * 64 + 64],
                                        exp_sb[:, esl_ * 1024 + h * 512: esl_ * 1024 + (h + 1) * 512],
                                        start=(kt_ == 0), stop=(kt_ == lastkt_),
                                        skip_group_check=True),
                                        waits=([evex_, v_ready] + ([yT_free] if kt_ == 0 else [])) if h == 0 else (),
                                        sig=(h == 1))
                                exp_slot_ev[esl_] = evav
                                if kt_ == lastkt_:
                                    qr_slot_free[cq_ % 2] = evnum_
                                    evy = p.op("V", lambda eng, cq_=cq_, ysl=ysl: eng.tensor_copy(
                                        y_stage[:, ysl * S + cq_ * 512: ysl * S + (cq_ + 1) * 512],
                                        yT_ps[:]),
                                        waits=[evav] + ([y_stage_free[ysl]] if cq_ == 0 else []),
                                        sig=True)
                                    yT_free = evy
                                    y_evac[cq_] = evy

                            for j, (cq, kt) in enumerate(num_units):
                                qsl = cq % 2
                                if kt == 0:
                                    evsh = p.op("V", lambda eng, cq=cq, qsl=qsl: eng.stream_shuffle(
                                        qs_tmp[:], qT[:, cq * 512:(cq + 1) * 512],
                                        [i ^ 1 for i in range(32)]),
                                        waits=[q_ready, qr_slot_free[qsl]],
                                        force=[q_ready], sig=True)
                                    p.op("V", lambda eng, cq=cq, qsl=qsl: eng.tensor_mul(
                                        qr_ch[:, qsl * 512:(qsl + 1) * 512],
                                        ctab[:, cq * 512:(cq + 1) * 512],
                                        qT[:, cq * 512:(cq + 1) * 512]))
                                    p.op("V", lambda eng, cq=cq: eng.tensor_mul(
                                        rope_tmp[:], stab[:, cq * 512:(cq + 1) * 512],
                                        qs_tmp[:]), force=[evsh])
                                    qr_ev[qsl] = p.op("V", lambda eng, qsl=qsl: eng.tensor_add(
                                        qr_ch[:, qsl * 512:(qsl + 1) * 512],
                                        qr_ch[:, qsl * 512:(qsl + 1) * 512],
                                        rope_tmp[:]), sig=True)
                                dlim = ((j + 1) * nd) // nn
                                piece_evs = []
                                while di < dlim:
                                    h, qt, pc = den_units[di]
                                    dsl = di % 2
                                    ev = None
                                    for half in range(DEN_W // 512):
                                        kc = pc * (DEN_W // 512) + half
                                        ev = p.op("T", lambda eng, h=h, qt=qt, kc=kc, half=half, dsl=dsl: eng.matmul(
                                            den_ps[:, dsl * DEN_W + half * 512: dsl * DEN_W + (half + 1) * 512],
                                            qT[64 * h: 64 * h + 64, qt * 128:(qt + 1) * 128],
                                            kT[64 * h: 64 * h + 64, kc * 512:(kc + 1) * 512],
                                            start=True, stop=True),
                                            waits=[q_ready, k_ready, den_slot_ev[dsl]] if half == 0 else (),
                                            sig=(half == DEN_W // 512 - 1))
                                    col = pc * (2 * NKT) + h * NKT + qt
                                    piece_evs.append((dsl, ev, col))
                                    di += 1
                                diag = kt >= 4 * cq
                                ev = None
                                for h in range(2):
                                    ev = p.op("T", lambda eng, h=h, kt=kt, qsl=qsl: eng.matmul(
                                        num_ps[:, h * 512:(h + 1) * 512],
                                        krT[64 * h:64 * h + 64, kt * 128:(kt + 1) * 128],
                                        qr_ch[64 * h:64 * h + 64, qsl * 512:(qsl + 1) * 512],
                                        start=True, stop=not diag, skip_group_check=True),
                                        waits=[ev_ropek, qr_ev[qsl], num_free] if h == 0 else (),
                                        sig=(h == 1 and not diag))
                                    if diag:
                                        tq = kt - 4 * cq
                                        ev = p.op("T", lambda eng, h=h, tq=tq: eng.matmul(
                                            num_ps[:, h * 512:(h + 1) * 512],
                                            identb[:], masks[:, tq * 512:(tq + 1) * 512],
                                            start=False, stop=True, skip_group_check=True),
                                            sig=(h == 1))
                                evnum = ev
                                esl = j % 2
                                evex = p.op("S", lambda eng, esl=esl: eng.activation(
                                    exp_sb[:, esl * 1024:(esl + 1) * 1024], num_ps[:],
                                    AF.Exp, scale=0.125),
                                    waits=[evnum, exp_slot_ev[esl]], sig=True)
                                num_free = evex
                                for dsl, evd, col in piece_evs:
                                    evs = p.op("S", lambda eng, col=col, dsl=dsl: eng.activation(
                                        t_f32[:, 0:DEN_W], den_ps[:, dsl * DEN_W:(dsl + 1) * DEN_W],
                                        AF.Exp, scale=0.125,
                                        accum_out=den_acc[:, col:col + 1]),
                                        waits=[evd, den_acc_free], sig=True)
                                    den_slot_ev[dsl] = evs
                                if pend is not None:
                                    emit_av(pend)
                                pend = (cq, kt, esl, evex, evnum)
                            emit_av(pend)
                            prev_post = (ysl, y_evac, [den_slot_ev[0], den_slot_ev[1]])
                    bank_gate = p.join_cv()
                    if debug_stage == "pairs" and b == 0:
                        dbg_dump("qT", qT[:], p.join_cv())
                        dbg_dump("kT", kT[:], p.join_cv())
                        dbg_dump("krT", krT[:], p.join_cv())
                        dbg_dump("v", v_sb[:], p.join_cv())
                        dbg_dump("denacc", den_acc[:], p.join_cv())
                        dbg_dump("densum", den_sum[:], p.join_cv())
                        dbg_dump("deninvT", den_invT[:], p.join_cv())
                        dbg_dump("denscr", den_scr, [y_written[NPAIR - 1]])
                        dbg_dump("yscr", y_scr, [y_written[NPAIR - 1]])
                        dbg_finish()
                        return nc
                    p.flush(block)

                with nc.psum_tensor(_nm("wout_ps"), [128, 4 * 1024], F32) as wout_ps:
                    yst_free = [None, None]
                    wout_ps_free = None
                    x_f32_free = None
                    for g in range(NT // 4):
                        ev_group = None
                        for pr in range(NPAIR):
                            ysl = pr % 2
                            evl = p.dma("DA", yst_sb[:, ysl * 512:(ysl + 1) * 512],
                                        y_scr[pr, :, g * 512:(g + 1) * 512],
                                        waits=[y_written[pr], yst_free[ysl]])
                            evm = None
                            for tsub in range(4):
                                for nh in range(2):
                                    w3 = [evl, ev_woutcast] + bank_gate
                                    if pr == 0 and tsub == 0 and nh == 0:
                                        w3.append(wout_ps_free)
                                    evm = p.op("T", lambda eng, pr=pr, tsub=tsub, nh=nh, ysl=ysl: eng.matmul(
                                        wout_ps[:, tsub * 1024 + nh * 512: tsub * 1024 + (nh + 1) * 512],
                                        yst_sb[:, ysl * 512 + tsub * 128: ysl * 512 + (tsub + 1) * 128],
                                        wout_sb[:, pr * D + nh * 512: pr * D + (nh + 1) * 512],
                                        start=(pr == 0), stop=(pr == NPAIR - 1),
                                        skip_group_check=True),
                                        waits=w3 if (tsub == 0 and nh == 0) else (),
                                        sig=(tsub == 3 and nh == 1))
                            yst_free[ysl] = evm
                            ev_group = evm
                        ep_last = None
                        for tsub in range(4):
                            tt = g * 4 + tsub
                            evx = p.dma("DS", x_f32[:],
                                        xin[b, tt * 128:(tt + 1) * 128, :],
                                        waits=[x_f32_free])
                            p.op("V", lambda eng, tsub=tsub: eng.tensor_add(
                                t_f32[:], wout_ps[:, tsub * 1024:(tsub + 1) * 1024],
                                x_f32[:]),
                                waits=[ev_group, evx])
                            ev1 = p.op("V", lambda eng: eng.tensor_add(
                                t_f32[:], t_f32[:], bout_bc[:]), sig=True)
                            x_f32_free = ev1
                            _f1 = ev1
                            if debug_stage == "woutpre":
                                evw = p.dma("DS",
                                            x1_scr[b * S + tt * 128: b * S + (tt + 1) * 128, :],
                                            t_f32[:], waits=[ev1])
                                ep_last = evw
                                x1_done.append(evw)
                                ev7 = evw
                                continue
                            evrd = p.op("V", lambda eng: eng.reduce_sum(
                                stat[:, 0:1], t_f32[:], axis=AX.X), force=[_f1],
                                sig=True)
                            evmu = p.op("V", lambda eng: eng.tensor_scalar_mul(
                                stat[:, 1:2], stat[:, 0:1], 1.0 / D), force=[evrd],
                                sig=True)
                            ev2 = p.op("V", lambda eng: eng.tensor_scalar(
                                t_f32[:], t_f32[:], stat[:, 1:2], None,
                                ALU.subtract), force=[evmu], sig=True)
                            ev3 = p.op("S", lambda eng: eng.activation(
                                trash_b[:, 0:D], t_f32[:], AF.Square,
                                accum_out=stat[:, 2:3]), waits=[ev2], sig=True)
                            ev4 = ev3
                            if debug_stage == "wout" and b == 0:
                                p.op("V", lambda eng, tt=tt: eng.tensor_copy(
                                    lndbg[:, tt:tt + 1], stat[:, 2:3]), waits=[ev3])
                            ev5 = p.op("S", lambda eng: eng.activation(
                                stat[:, 4:5], stat[:, 2:3], AF.Ln,
                                scale=1.0 / D, bias=eps_sb[:, 0:1]),
                                waits=[ev_eps], force=[ev3], sig=True)
                            ev6 = p.op("S", lambda eng: eng.activation(
                                stat[:, 5:6], stat[:, 4:5], AF.Exp, scale=-0.5),
                                sig=True)
                            if debug_stage == "wout" and b == 0:
                                p.op("V", lambda eng, tt=tt: eng.tensor_copy(
                                    lndbg[:, 64 + tt:64 + tt + 1], stat[:, 5:6]),
                                    waits=[ev6])
                            ev7 = p.op("V", lambda eng: eng.tensor_scalar(
                                t_f32[:], t_f32[:], stat[:, 5:6], None, ALU.mult),
                                waits=[ev6], sig=True)
                            evw = p.dma("DS",
                                        x1_scr[b * S + tt * 128: b * S + (tt + 1) * 128, :],
                                        t_f32[:], waits=[ev7])
                            ep_last = evw
                            x1_done.append(evw)
                        wout_ps_free = ev7
                    bank_gate = p.join_cv()
                    if debug_stage in ("wout", "woutpre") and b == 0:
                        dbg_dump("x1scr", x1_scr, [x1_done[-1]])
                        if debug_stage == "wout":
                            dbg_dump("lndbg", lndbg[:], p.join_cv())
                        dbg_finish()
                        return nc
                    p.flush(block)
            ev_x1_all = list(x1_done)
            p1_gate = p.join_cv()

        with ExitStack() as c2:
            sb2 = lambda n, sh, d_: c2.enter_context(nc.sbuf_tensor(n, sh, d_))
            x1t = sb2("x1t", [128, 2 * D], F32)
            tt_f = sb2("tt_f", [128, D], F32)
            trash2 = sb2("trash2", [128, D], F32)
            stat2 = sb2("stat2", [128, 8], F32)
            xfT_f = sb2("xfT_f", [128, 8 * 128], F32)
            xfn_b4 = sb2("xfn_b4", [128, 8 * 512], BF16)
            logit_sb = sb2("logit_sb", [128, 8], F32)
            eq_sb = sb2("eq_sb", [128, 8], F32)
            we8 = sb2("we8", [128, 8], F32)
            weT_all = sb2("weT_all", [8, TOK], F32)

            with ExitStack() as cp2:
                trf_ps = cp2.enter_context(nc.psum_tensor(_nm("trf_ps"), [128, 2 * 512], F32))
                log_ps = cp2.enter_context(nc.psum_tensor(_nm("log_ps"), [128, 8], F32))
                tr8_ps = cp2.enter_context(nc.psum_tensor(_nm("tr8_ps"), [128, 128], F32))

                x1t_free = [None, None]
                trf_free = [None, None]
                log_free = None
                tr8_free = None
                stage_ev = None
                for tt in range(NTT):
                    sl = tt % 2
                    evx = p.dma("DS", x1t[:, sl * D:(sl + 1) * D],
                                x1_scr[tt * 128:(tt + 1) * 128, :],
                                waits=[x1t_free[sl]] + (ev_x1_all[-1:] if tt == 0 else []))
                    evrd = p.op("V", lambda eng, sl=sl: eng.reduce_sum(
                        stat2[:, 0:1], x1t[:, sl * D:(sl + 1) * D], axis=AX.X),
                        waits=[evx] + p1_gate, sig=True)
                    evmu = p.op("V", lambda eng: eng.tensor_scalar_mul(
                        stat2[:, 1:2], stat2[:, 0:1], 1.0 / D), force=[evrd],
                        sig=True)
                    ev2 = p.op("V", lambda eng, sl=sl: eng.tensor_scalar(
                        tt_f[:], x1t[:, sl * D:(sl + 1) * D], stat2[:, 1:2], None,
                        ALU.subtract), force=[evmu], sig=True)
                    x1t_free[sl] = ev2
                    ev3 = p.op("S", lambda eng: eng.activation(
                        trash2[:], tt_f[:], AF.Square, accum_out=stat2[:, 2:3]),
                        waits=[ev2] + p1_gate, sig=True)
                    ev4 = ev3
                    ev5 = p.op("S", lambda eng: eng.activation(
                        stat2[:, 4:5], stat2[:, 2:3], AF.Ln,
                        scale=1.0 / D, bias=eps_sb[:, 0:1]),
                        waits=[ev_eps], force=[ev3], sig=True)
                    ev6 = p.op("S", lambda eng: eng.activation(
                        stat2[:, 5:6], stat2[:, 4:5], AF.Exp, scale=-0.5), sig=True)
                    p.op("V", lambda eng: eng.tensor_scalar(
                        tt_f[:], tt_f[:], stat2[:, 5:6], None, ALU.mult),
                        waits=[ev6])
                    p.op("V", lambda eng: eng.tensor_mul(tt_f[:], tt_f[:], lnw_bc[:]))
                    ev7 = p.op("V", lambda eng: eng.tensor_add(
                        tt_f[:], tt_f[:], lnb_bc[:]), sig=True)
                    evlog = None
                    for c in range(8):
                        ts = c % 2
                        evt = p.op("T", lambda eng, c=c, ts=ts: eng.transpose(
                            trf_ps[:, ts * 512: ts * 512 + 128],
                            tt_f[:, c * 128:(c + 1) * 128], identf[:]),
                            waits=[ev7, trf_free[ts]] + (p1_gate if tt == 0 else []),
                            sig=True)
                        evcp = p.op("V", lambda eng, c=c, ts=ts: eng.tensor_copy(
                            xfT_f[:, c * 128:(c + 1) * 128],
                            trf_ps[:, ts * 512: ts * 512 + 128]),
                            waits=[evt], sig=True)
                        evb = p.op("V", lambda eng, c=c, tt=tt: eng.tensor_copy(
                            xfn_b4[:, c * 512 + (tt % 4) * 128: c * 512 + (tt % 4 + 1) * 128],
                            xfT_f[:, c * 128:(c + 1) * 128]),
                            waits=[stage_ev if tt % 4 == 0 else None], sig=True)
                        trf_free[ts] = evb
                        evlog = p.op("T", lambda eng, c=c: eng.matmul(
                            log_ps[:], xfT_f[:, c * 128:(c + 1) * 128],
                            routsb[:, c * 8:(c + 1) * 8],
                            start=(c == 0), stop=(c == 7)),
                            waits=[evcp] + ([log_free] if c == 0 else []),
                            sig=(c == 7))
                    if tt % 4 == 3:
                        evst = [p.dma("DS",
                                      xfn_scr[c, :, (tt - 3) * 128:(tt + 1) * 128],
                                      xfn_b4[:, c * 512:(c + 1) * 512],
                                      waits=[evb]) for c in range(8)]
                        stage_ev = evst[-1]
                    evlc = p.op("V", lambda eng: eng.tensor_copy(
                        logit_sb[:], log_ps[:]), waits=[evlog], sig=True)
                    log_free = evlc
                    evrm = p.op("V", lambda eng: eng.reduce_max(
                        stat2[:, 6:7], logit_sb[:], axis=AX.X), force=[evlc],
                        sig=True)
                    ev8 = p.op("V", lambda eng: eng.tensor_scalar(
                        eq_sb[:], logit_sb[:], stat2[:, 6:7], None, ALU.is_equal),
                        force=[evrm], sig=True)
                    ev9 = p.op("S", lambda eng: eng.activation(
                        stat2[:, 7:8], stat2[:, 6:7], AF.Exp, scale=-1.0),
                        waits=[ev8], sig=True)
                    ev9b = p.op("V", lambda eng: eng.tensor_scalar_add(
                        stat2[:, 7:8], stat2[:, 7:8], 1.0), waits=[ev9], sig=True)
                    ev10 = p.op("V", lambda eng: eng.reciprocal(
                        stat2[:, 7:8], stat2[:, 7:8]), force=[ev9b], sig=True)
                    ev11 = p.op("V", lambda eng: eng.tensor_scalar(
                        we8[:], eq_sb[:], stat2[:, 7:8], None, ALU.mult),
                        force=[ev10], sig=True)
                    evt8 = p.op("T", lambda eng: eng.transpose(
                        tr8_ps[0:8, :], we8[:], identf[:]),
                        waits=[ev11, tr8_free], sig=True)
                    tr8_free = p.op("V", lambda eng, tt=tt: eng.tensor_copy(
                        weT_all[:, tt * 128:(tt + 1) * 128], tr8_ps[0:8, 0:128]),
                        waits=[evt8], sig=True)
                ev_wescr = p.dma("DS", we_scr, weT_all[:], waits=[tr8_free])
                if debug_stage == "p2":
                    dbg_dump("weT", weT_all[:], p.join_cv())
                    dbg_dump("logit", logit_sb[:], p.join_cv())
                    dbg_dump("eq", eq_sb[:], p.join_cv())
                    dbg_dump("we8", we8[:], p.join_cv())
                    dbg_dump("stat2", stat2[:], p.join_cv())
                    dbg_dump("wescr", we_scr, [ev_wescr])
                    dbg_finish()
                    return nc
                p2_gate = p.join_cv()
                p.flush(block)

        with ExitStack() as c3:
            sb3 = lambda n, sh, d_: c3.enter_context(nc.sbuf_tensor(n, sh, d_))
            xfnT8 = sb3("xfnT8", [128, 8 * ETOK], BF16)
            xfeT8 = sb3("xfeT8", [128, 8 * ETOK], BF16)
            hmidT = sb3("hmidT", [128, 16 * ETOK], BF16)
            routed8 = sb3("routed8", [128, ETT * D], BF16)
            g_buf = sb3("g_buf", [128, 2 * 8 * 512], BF16)
            u_buf = sb3("u_buf", [128, 2 * 8 * 512], BF16)
            d_buf = sb3("d_buf", [128, 16 * 1024], BF16)
            bc3_sb = sb3("bc3_sb", [128, ETOK], F32)
            silu_sb = sb3("silu_sb", [128, 2 * 512], BF16)
            we_rows8 = sb3("we_rows8", [1, 8 * ETOK], F32)

            with ExitStack() as cp3:
                hg_ps = cp3.enter_context(nc.psum_tensor(_nm("hg_ps"), [128, 2 * 512], F32))
                hu_ps = cp3.enter_context(nc.psum_tensor(_nm("hu_ps"), [128, 2 * 512], F32))
                rt_ps = cp3.enter_context(nc.psum_tensor(_nm("rt_ps"), [128, 1024], F32))
                bc3_ps = cp3.enter_context(nc.psum_tensor(_nm("bc3_ps"), [128, 2 * 512], F32))

                xfn8_ev = None
                gu_used = {}
                d_used = None
                d_ev = None
                hg_free = [None, None]
                hu_free = [None, None]
                rt_free = None
                bc3_free = [None, None]
                silu_free = [None, None]
                hmid_used = None
                gateup_done = None
                routed_wr = None

                NMG = 4
                we8_lastuse = None
                for q in range(NEI):
                    evx8 = [p.dma("DS", xfnT8[:, c * ETOK:(c + 1) * ETOK],
                                  xfn_scr[c, :, q * ETOK:(q + 1) * ETOK],
                                  waits=[stage_ev, gateup_done])
                            for c in range(8)]
                    xfn8_ev = evx8[-1]
                    ev_we8 = p.dma(
                        "DS", we_rows8[0:1, :].rearrange("o (e n) -> o e n", e=8),
                        we_scr[:, q * ETOK:(q + 1) * ETOK],
                        waits=[ev_wescr, we8_lastuse])
                    for e in range(NE + 1):
                        wd = p2_gate + ([] if d_used is None else [d_used])
                        d_ev = p.dma("DG", d_buf[:].rearrange("p (m n) -> p m n", n=1024),
                                     down_in[e].rearrange("(m p) n -> p m n", p=128),
                                     waits=wd)
                        if e < NE:
                            evbc = []
                            for cc in range(ECH):
                                csl = cc % 2
                                evb1 = p.op("T", lambda eng, e=e, cc=cc, csl=csl: eng.matmul(
                                    bc3_ps[:, csl * 512:(csl + 1) * 512],
                                    ones_f[0:1, :],
                                    we_rows8[0:1, e * ETOK + cc * 512: e * ETOK + (cc + 1) * 512],
                                    start=True, stop=True),
                                    waits=[ev_we8, bc3_free[csl]] + (p2_gate if q == 0 and e == 0 else []),
                                    sig=True)
                                we8_lastuse = evb1
                                evb2 = p.op("V", lambda eng, cc=cc, csl=csl: eng.tensor_copy(
                                    bc3_sb[:, cc * 512:(cc + 1) * 512],
                                    bc3_ps[:, csl * 512:(csl + 1) * 512]),
                                    waits=[evb1], sig=True)
                                bc3_free[csl] = evb2
                                evbc.append(evb2)
                            xfe_ev = None
                            for c in range(8):
                                xfe_ev = p.op("V", lambda eng, c=c: eng.tensor_mul(
                                    xfeT8[:, c * ETOK:(c + 1) * ETOK],
                                    xfnT8[:, c * ETOK:(c + 1) * ETOK],
                                    bc3_sb[:]),
                                    waits=[evbc[-1], xfn8_ev, gateup_done], sig=True)
                            src8 = xfeT8
                        else:
                            xfe_ev = xfn8_ev
                            src8 = xfnT8
                        hm_ev = None
                        for mg in range(NMG):
                            gsl = (e * NMG + mg) % 2
                            wg = p2_gate + [gu_used.get(("g", gsl))]
                            evg = p.dma("DG", g_buf[:, gsl * 4096:(gsl + 1) * 4096].rearrange(
                                "p (k n) -> p k n", n=512),
                                gate_in[e].rearrange("(k p) i -> p k i", p=128)[
                                    :, :, mg * 512:(mg + 1) * 512],
                                waits=wg)
                            evu = p.dma("DG", u_buf[:, gsl * 4096:(gsl + 1) * 4096].rearrange(
                                "p (k n) -> p k n", n=512),
                                up_in[e].rearrange("(k p) i -> p k i", p=128)[
                                    :, :, mg * 512:(mg + 1) * 512],
                                waits=p2_gate + [gu_used.get(("u", gsl))])
                            for mm in range(4):
                                m = mg * 4 + mm
                                for cc in range(ECH):
                                    hsl = (m * ECH + cc) % 2
                                    evh = evu2 = None
                                    for k in range(8):
                                        evh = p.op("T", lambda eng, k=k, m=m, mm=mm, cc=cc, hsl=hsl, gsl=gsl, src8=src8: eng.matmul(
                                            hg_ps[:, hsl * 512:(hsl + 1) * 512],
                                            g_buf[:, gsl * 4096 + k * 512 + mm * 128: gsl * 4096 + k * 512 + (mm + 1) * 128],
                                            src8[:, k * ETOK + cc * 512: k * ETOK + cc * 512 + 512],
                                            start=(k == 0), stop=(k == 7)),
                                            waits=[evg, xfe_ev, hg_free[hsl]] if k == 0 else (),
                                            sig=(k == 7))
                                    for k in range(8):
                                        evu2 = p.op("T", lambda eng, k=k, m=m, mm=mm, cc=cc, hsl=hsl, gsl=gsl, src8=src8: eng.matmul(
                                            hu_ps[:, hsl * 512:(hsl + 1) * 512],
                                            u_buf[:, gsl * 4096 + k * 512 + mm * 128: gsl * 4096 + k * 512 + (mm + 1) * 128],
                                            src8[:, k * ETOK + cc * 512: k * ETOK + cc * 512 + 512],
                                            start=(k == 0), stop=(k == 7)),
                                            waits=[evu, xfe_ev, hu_free[hsl]] if k == 0 else (),
                                            sig=(k == 7))
                                    evsl = p.op("S", lambda eng, hsl=hsl: eng.activation(
                                        silu_sb[:, hsl * 512:(hsl + 1) * 512],
                                        hg_ps[:, hsl * 512:(hsl + 1) * 512],
                                        AF.Silu),
                                        waits=[evh, silu_free[hsl]], sig=True)
                                    hg_free[hsl] = evsl
                                    evhm = p.op("V", lambda eng, m=m, cc=cc, hsl=hsl: eng.tensor_mul(
                                        hmidT[:, m * ETOK + cc * 512: m * ETOK + cc * 512 + 512],
                                        silu_sb[:, hsl * 512:(hsl + 1) * 512],
                                        hu_ps[:, hsl * 512:(hsl + 1) * 512]),
                                        waits=[evsl, evu2, hmid_used], sig=True)
                                    hu_free[hsl] = evhm
                                    silu_free[hsl] = evhm
                                    hm_ev = evhm
                            gu_used[("g", gsl)] = evh
                            gu_used[("u", gsl)] = evu2
                        gateup_done = hm_ev
                        dn_last = None
                        for tt in range(ETT):
                            evd = None
                            for m in range(16):
                                for nh in range(2):
                                    evd = p.op("T", lambda eng, m=m, tt=tt, nh=nh: eng.matmul(
                                        rt_ps[:, nh * 512:(nh + 1) * 512],
                                        hmidT[:, m * ETOK + tt * 128: m * ETOK + (tt + 1) * 128],
                                        d_buf[:, m * 1024 + nh * 512: m * 1024 + (nh + 1) * 512],
                                        start=(m == 0), stop=(m == 15),
                                        skip_group_check=True),
                                        waits=[hm_ev, d_ev, rt_free] if (m == 0 and nh == 0) else (),
                                        sig=(m == 15 and nh == 1))
                            if e == 0:
                                evrt = p.op("V", lambda eng, tt=tt: eng.tensor_copy(
                                    routed8[:, tt * D:(tt + 1) * D], rt_ps[:]),
                                    waits=[evd, routed_wr], sig=True)
                            else:
                                evrt = p.op("V", lambda eng, tt=tt: eng.tensor_add(
                                    routed8[:, tt * D:(tt + 1) * D],
                                    routed8[:, tt * D:(tt + 1) * D], rt_ps[:]),
                                    waits=[evd], sig=True)
                            rt_free = evrt
                            dn_last = evrt
                        d_used = evd
                        hmid_used = evd
                    routed_wr = p.dma(
                        "DS",
                        routed_scr[q * ETOK:(q + 1) * ETOK, :].rearrange(
                            "(t p) d -> p t d", p=128),
                        routed8[:].rearrange("p (t d) -> p t d", d=D),
                        waits=[dn_last])
                p3_gate = p.join_cv()
                ev_routed_all = routed_wr
                if debug_stage == "p3":
                    dbg_dump("x1scr", x1_scr, [ev_x1_all[-1]])
                    dbg_dump("xfnscr", xfn_scr, [stage_ev])
                    dbg_dump("wescr", we_scr, [ev_wescr])
                    dbg_dump("routedscr", routed_scr, [routed_wr])
                    dbg_finish()
                    return nc
                p.flush(block)

        with ExitStack() as c4:
            sb4 = lambda n, sh, d_: c4.enter_context(nc.sbuf_tensor(n, sh, d_))
            x1f = sb4("x1f", [128, 2 * D], F32)
            rtf = sb4("rtf", [128, 2 * D], BF16)
            tf4 = sb4("tf4", [128, D], F32)
            trash4 = sb4("trash4", [128, D], F32)
            stat4 = sb4("stat4", [128, 8], F32)

            x1f_free = [None, None]
            rtf_free = [None, None]
            for tt in range(NTT):
                sl = tt % 2
                b = tt // NT
                r0 = (tt % NT) * 128
                evx = p.dma("DS", x1f[:, sl * D:(sl + 1) * D],
                            x1_scr[tt * 128:(tt + 1) * 128, :],
                            waits=[x1f_free[sl]] + ([ev_routed_all] if tt == 0 else []))
                evr = p.dma("DA", rtf[:, sl * D:(sl + 1) * D],
                            routed_scr[tt * 128:(tt + 1) * 128, :],
                            waits=[rtf_free[sl]] + ([ev_routed_all] if tt == 0 else []))
                ev1 = p.op("V", lambda eng, sl=sl: eng.scalar_tensor_tensor(
                    tf4[:], x1f[:, sl * D:(sl + 1) * D], 2.0,
                    rtf[:, sl * D:(sl + 1) * D], ALU.mult, ALU.add),
                    waits=[evx, evr] + (p3_gate if tt == 0 else []), sig=True)
                x1f_free[sl] = ev1
                rtf_free[sl] = ev1
                evrd = p.op("V", lambda eng: eng.reduce_sum(
                    stat4[:, 0:1], tf4[:], axis=AX.X), force=[ev1], sig=True)
                evmu = p.op("V", lambda eng: eng.tensor_scalar_mul(
                    stat4[:, 1:2], stat4[:, 0:1], 1.0 / D), force=[evrd],
                    sig=True)
                ev2 = p.op("V", lambda eng: eng.tensor_scalar(
                    tf4[:], tf4[:], stat4[:, 1:2], None, ALU.subtract),
                    force=[evmu], sig=True)
                ev3 = p.op("S", lambda eng: eng.activation(
                    trash4[:], tf4[:], AF.Square, accum_out=stat4[:, 2:3]),
                    waits=[ev2] + (p3_gate if tt == 0 else []), sig=True)
                ev4 = ev3
                ev5 = p.op("S", lambda eng: eng.activation(
                    stat4[:, 4:5], stat4[:, 2:3], AF.Ln,
                    scale=1.0 / D, bias=eps_sb[:, 0:1]),
                    waits=[ev_eps], force=[ev3], sig=True)
                ev6 = p.op("S", lambda eng: eng.activation(
                    stat4[:, 5:6], stat4[:, 4:5], AF.Exp, scale=-0.5),
                    force=[ev5], sig=True)
                ev7 = p.op("V", lambda eng: eng.tensor_scalar(
                    tf4[:], tf4[:], stat4[:, 5:6], None, ALU.mult),
                    waits=[ev6], sig=True)
                p.dma("DS", out[b, r0:r0 + 128, :], tf4[:], waits=[ev7])
            p.op("SY", lambda eng: eng.nop(),
                 waits=[(ch, p.count[ch]) for ch in DMA_CH if p.count[ch] > 0])
            p.flush(block)
    return nc


def _pack_inputs(ins, S=4096):
    f32 = np.float32
    bf16 = ml_dtypes.bfloat16
    x = np.ascontiguousarray(np.asarray(ins["L0_attn_x"], f32))
    w_qkv = np.asarray(ins["L0_attn_w_qkv"], f32)
    wqkvT = np.ascontiguousarray(w_qkv.T)
    wpk = np.empty((NPAIR, 128, 8 * 384), f32)
    for pr in range(NPAIR):
        t = np.concatenate([wqkvT[:, 128 * pr:128 * pr + 128],
                            wqkvT[:, D + 128 * pr:D + 128 * pr + 128],
                            wqkvT[:, 2 * D + 128 * pr:2 * D + 128 * pr + 128]],
                           axis=1)
        wpk[pr] = t.reshape(8, 128, 384).transpose(1, 0, 2).reshape(128, 8 * 384)
    w_out = np.asarray(ins["L0_attn_w_out"], f32)
    woutpk = np.ascontiguousarray(
        w_out.T.reshape(8, 128, D).transpose(1, 0, 2).reshape(128, 8 * D))
    router = np.asarray(ins["L0_ffn_router_w"], f32)
    routpk = np.ascontiguousarray(
        router.reshape(8, 128, 8).transpose(1, 0, 2).reshape(128, 64))
    rows = np.stack([np.asarray(ins["L0_attn_b_out"], f32),
                     np.asarray(ins["L0_ffn_ln_w"], f32),
                     np.asarray(ins["L0_ffn_ln_b"], f32)], axis=0)
    i_idx = (np.arange(128) % 64) // 2
    theta = 10000.0 ** (-2.0 * (i_idx - 1.0) / 64.0)
    tpos = np.arange(S, dtype=np.float64)
    ang = tpos[None, :] * theta[:, None]
    ctab = np.cos(ang).astype(bf16)
    sgn = np.where(np.arange(128) % 2 == 0, 1.0, -1.0)[:, None]
    stab = (np.sin(ang) * sgn).astype(bf16)
    pgrid = np.arange(128)[:, None]
    ngrid = np.arange(512)[None, :]
    masks = np.concatenate(
        [np.where(ngrid >= 128 * tq + pgrid, 0.0, NEG) for tq in range(4)],
        axis=1).astype(bf16)
    identb = np.eye(128, dtype=bf16)
    identf = np.eye(128, dtype=f32)
    gate = np.concatenate([np.asarray(ins["L0_ffn_gate_w"], f32),
                           np.asarray(ins["L0_ffn_sh_gate"], f32)[None]], axis=0)
    up = np.concatenate([np.asarray(ins["L0_ffn_up_w"], f32),
                         np.asarray(ins["L0_ffn_sh_up"], f32)[None]], axis=0)
    down = np.concatenate([np.asarray(ins["L0_ffn_down_w"], f32),
                           np.asarray(ins["L0_ffn_sh_down"], f32)[None]], axis=0)
    shared = {"wpk_in": wpk, "woutpk": woutpk, "routpk": routpk, "rows_in": rows,
              "ctab_in": np.ascontiguousarray(ctab),
              "stab_in": np.ascontiguousarray(stab),
              "masks_in": np.ascontiguousarray(masks),
              "identb_in": identb, "identf_in": identf,
              "gate_in": np.ascontiguousarray(gate),
              "up_in": np.ascontiguousarray(up),
              "down_in": np.ascontiguousarray(down)}
    n_b = x.shape[0]
    in_maps = []
    for c in range(n_b // B_CORE):
        m = dict(shared)
        m["xin"] = np.ascontiguousarray(x[c * B_CORE:(c + 1) * B_CORE])
        in_maps.append(m)
    return in_maps


def kernel(**inputs):
    S = int(np.asarray(inputs["L0_attn_x"]).shape[1])
    in_maps = _pack_inputs(inputs, S=S)
    nc = build_core_program(S=S)
    res = run_bass_kernel_spmd(nc, in_maps, core_ids=list(range(len(in_maps))))
    outs = [np.asarray(r["out"], dtype=np.float32) for r in res.results]
    return np.concatenate(outs, axis=0)


# version 11
# speedup vs baseline: 1.0395x; 1.0365x over previous
import numpy as np
import ml_dtypes
import concourse.bass as bass
import concourse.mybir as mybir
from concourse.bass_utils import run_bass_kernel_spmd
from contextlib import ExitStack

F32 = mybir.dt.float32
BF16 = mybir.dt.bfloat16
AF = mybir.ActivationFunctionType
ALU = mybir.AluOpType
AX = mybir.AxisListType

D = 1024
H = 16
DH = 64
NPAIR = 8
NE = 8
I_FF = 2048
EPS = 1e-5
B_CORE = 2
NCORES = 8
NEG = -1.0e30

DMA_CH = ("DS", "DG", "DA")
CH_ENGINE = {"T": "tensor", "V": "vector", "S": "scalar", "G": "gpsimd",
             "SY": "sync", "DS": "sync", "DG": "gpsimd", "DA": "scalar"}


class Prog:

    def __init__(self, nc, es):
        self.nc = nc
        self.ops = []
        self.count = {ch: 0 for ch in CH_ENGINE}
        self.sems = {ch: es.enter_context(nc.semaphore("sem_" + ch))
                     for ch in CH_ENGINE if ch != "SY"}
        self.seen = {}

    def op(self, ch, fn, waits=(), sig=False, force=()):
        waits = [w for w in waits if w is not None]
        force = [w for w in force if w is not None]
        for (wc, wv) in waits + force:
            assert wv <= self.count[wc], (wc, wv, self.count[wc])
        inc = 16 if ch in DMA_CH else (1 if sig else 0)
        self.ops.append((ch, fn, tuple(waits), inc, tuple(force)))
        if inc:
            self.count[ch] += inc
            return (ch, self.count[ch])
        return None

    def dma(self, ch, out, in_, waits=()):
        assert ch in DMA_CH
        return self.op(ch, lambda eng, o=out, i=in_: eng.dma_start(out=o, in_=i),
                       waits=waits)

    def join_cv(self):
        return [(c, self.count[c]) for c in ("T", "V", "S") if self.count[c] > 0]

    def flush(self, block):
        by_eng = {}
        for rec in self.ops:
            by_eng.setdefault(CH_ENGINE[rec[0]], []).append(rec)
        self.ops = []
        for eng_name in ("sync", "gpsimd", "tensor", "vector", "scalar"):
            recs = by_eng.get(eng_name)
            if not recs:
                continue
            seen = self.seen.setdefault(eng_name, {c: 0 for c in CH_ENGINE})

            def emit(eng, recs=recs, seen=seen, eng_name=eng_name):
                for ch, fn, waits, inc, force in recs:
                    for wc, wv in waits:
                        if wc not in DMA_CH and CH_ENGINE[wc] == eng_name:
                            continue
                        if wv <= seen[wc]:
                            continue
                        eng.wait_ge(self.sems[wc], wv)
                        seen[wc] = wv
                    for wc, wv in force:
                        eng.wait_ge(self.sems[wc], wv)
                    inst = fn(eng)
                    if inc:
                        inst.then_inc(self.sems[ch], inc)

            getattr(block, eng_name)(emit)


def build_core_program(S=4096, debug=False, debug_stage=None):
    NT = S // 128
    NQC = S // 512
    NKT = S // 128
    TOK = B_CORE * S
    NTT = TOK // 128
    DEN_W = min(1024, S)
    NPD = S // DEN_W
    ETOK = min(1024, TOK)
    NEI = TOK // ETOK
    ETT = ETOK // 128
    ECH = ETOK // 512
    assert NT % 4 == 0

    nc = bass.Bass("TRN2", target_bir_lowering=False)
    dt = nc.dram_tensor
    _uid = [0]

    def _nm(s):
        _uid[0] += 1
        return f"{s}_{_uid[0]}"

    xin = dt("xin", [B_CORE, S, D], F32, kind="ExternalInput").ap()
    wpk_in = dt("wpk_in", [NPAIR, 128, 8 * 384], F32, kind="ExternalInput").ap()
    woutpk = dt("woutpk", [128, 8 * D], F32, kind="ExternalInput").ap()
    routpk = dt("routpk", [128, 8 * 8], F32, kind="ExternalInput").ap()
    rows_in = dt("rows_in", [3, D], F32, kind="ExternalInput").ap()
    ctab_in = dt("ctab_in", [128, S], BF16, kind="ExternalInput").ap()
    stab_in = dt("stab_in", [128, S], BF16, kind="ExternalInput").ap()
    masks_in = dt("masks_in", [128, 4 * 512], BF16, kind="ExternalInput").ap()
    identb_in = dt("identb_in", [128, 128], BF16, kind="ExternalInput").ap()
    identf_in = dt("identf_in", [128, 128], F32, kind="ExternalInput").ap()
    gate_in = dt("gate_in", [NE + 1, D, I_FF], F32, kind="ExternalInput").ap()
    up_in = dt("up_in", [NE + 1, D, I_FF], F32, kind="ExternalInput").ap()
    down_in = dt("down_in", [NE + 1, I_FF, D], F32, kind="ExternalInput").ap()

    out = dt("out", [B_CORE, S, D], F32, kind="ExternalOutput").ap()

    y_scr = dt("y_scr", [NPAIR, 128, S], BF16, kind="Internal").ap()
    x1_scr = dt("x1_scr", [TOK, D], F32, kind="Internal").ap()
    xfn_scr = dt("xfn_scr", [8, 128, TOK], BF16, kind="Internal").ap()
    routed_scr = dt("routed_scr", [TOK, D], BF16, kind="Internal").ap()
    we_scr = dt("we_scr", [8, TOK], F32, kind="Internal").ap()
    den_scr = dt("den_scr", [2, S], F32, kind="Internal").ap()
    if debug:
        dbg_x1 = dt("dbg_x1", [TOK, D], F32, kind="ExternalOutput").ap()
        dbg_y = dt("dbg_y", [NPAIR, 128, S], BF16, kind="ExternalOutput").ap()
        dbg_xfn = dt("dbg_xfn", [8, 128, TOK], BF16, kind="ExternalOutput").ap()
        dbg_we = dt("dbg_we", [8, TOK], F32, kind="ExternalOutput").ap()
        dbg_rt = dt("dbg_rt", [TOK, D], BF16, kind="ExternalOutput").ap()
        dbg_den = dt("dbg_den", [2, S], F32, kind="ExternalOutput").ap()

    with ExitStack() as ctx:
        p = Prog(nc, ctx)
        sbuf = lambda n, sh, d_: ctx.enter_context(nc.sbuf_tensor(n, sh, d_))
        block = ctx.enter_context(nc.Block())
        dbg_bufs = {}

        def dbg_dump(name, src_ap, waits):
            shp = list(src_ap.shape)
            dtp = src_ap.dtype
            t = nc.dram_tensor("dump_" + name, shp, dtp, kind="ExternalOutput").ap()
            p.dma("DS", t, src_ap, waits=waits)

        def dbg_finish():
            p.op("SY", lambda eng: eng.nop(),
                 waits=[(ch, p.count[ch]) for ch in DMA_CH if p.count[ch] > 0])
            p.flush(block)

        identb = sbuf("identb", [128, 128], BF16)
        identf = sbuf("identf", [128, 128], F32)
        ones_f = sbuf("ones_f", [1, 128], F32)
        routsb = sbuf("routsb", [128, 8 * 8], F32)
        eps_sb = sbuf("eps_sb", [128, 1], F32)
        c0 = ExitStack()
        sbuf0 = lambda n, sh, d_: c0.enter_context(nc.sbuf_tensor(n, sh, d_))
        row_tmp = sbuf0("row_tmp", [1, D], F32)
        bout_bc = sbuf0("bout_bc", [128, D], F32)
        lnw_bc = sbuf0("lnw_bc", [128, D], F32)
        lnb_bc = sbuf0("lnb_bc", [128, D], F32)

        ld_const = [p.dma("DS", t[:], s) for t, s in
                    ((identb, identb_in), (identf, identf_in), (routsb, routpk))]
        ev_ones = p.op("V", lambda eng: eng.memset(ones_f[:], 1.0), sig=True)
        ev_eps = p.op("V", lambda eng: eng.memset(eps_sb[:], EPS), sig=True)

        with nc.psum_tensor(_nm("bcpre_ps"), [128, 512], F32) as bcpre_ps:
            prev = None
            for ri, dst in ((0, bout_bc), (1, lnw_bc), (2, lnb_bc)):
                evr = p.dma("DS", row_tmp[:], rows_in[ri:ri + 1, :],
                            waits=[prev])
                for c in range(D // 512):
                    evm = p.op("T", lambda eng, c=c: eng.matmul(
                        bcpre_ps[:], ones_f[0:1, :],
                        row_tmp[0:1, c * 512:(c + 1) * 512],
                        start=True, stop=True),
                        waits=[evr, ev_ones, prev], sig=True)
                    prev = p.op("V", lambda eng, c=c, dst=dst: eng.tensor_copy(
                        dst[:, c * 512:(c + 1) * 512], bcpre_ps[:]),
                        waits=[evm], sig=True)
            p.flush(block)

        with ExitStack() as c1:
            sb1 = lambda n, sh, d_: c1.enter_context(nc.sbuf_tensor(n, sh, d_))
            xT = sb1("xT", [128, 8 * S], BF16)
            xnat_b = sb1("xnat_b", [128, D], BF16)
            qT = sb1("qT", [128, S], BF16)
            kT = sb1("kT", [128, S], BF16)
            qr_ch = sb1("qr_ch", [128, 2 * 512], BF16)
            krT = sb1("krT", [128, S], BF16)
            qs_tmp = sb1("qs_tmp", [128, 512], BF16)
            rope_tmp = sb1("rope_tmp", [128, 512], BF16)
            v_sb = sb1("v_sb", [128, S], BF16)
            y_stage = sb1("y_stage", [128, 2 * S], BF16)
            exp_sb = sb1("exp_sb", [128, 2 * 1024], BF16)
            trash_b = sb1("trash_b", [128, max(DEN_W, D)], BF16)
            den_acc = sb1("den_acc", [128, NPD * 2 * NKT], F32)
            den_sum = sb1("den_sum", [128, 2 * NKT], F32)
            den_invT = sb1("den_invT", [64, 128], F32)
            den_row_ch = sb1("den_row_ch", [1, 2 * 512], F32)
            bc_sb = sb1("bc_sb", [128, 512], F32)
            wpk_b = sb1("wpk_b", [128, 2 * 3072], BF16)
            wout_sb = sb1("wout_sb", [128, 8 * D], BF16)
            yst_sb = sb1("yst_sb", [128, 2 * 512], BF16)
            x_f32 = sb1("x_f32", [128, D], F32)
            t_f32 = sb1("t_f32", [128, D], F32)
            stat = sb1("stat", [128, 8], F32)
            lndbg = sb1("lndbg", [128, 3 * 32], F32)
            ctab = sb1("ctab", [128, S], BF16)
            stab = sb1("stab", [128, S], BF16)
            masks = sb1("masks", [128, 4 * 512], BF16)

            ld_tab = [p.dma("DS", ctab[:], ctab_in),
                      p.dma("DS", stab[:], stab_in),
                      p.dma("DS", masks[:], masks_in)]
            ld_const = ld_const + ld_tab
            ev_woutcast = p.dma("DG", wout_sb[:], woutpk)

            x1_done = []
            bank_gate = []
            wpkb_free = [None, None]
            for b in range(B_CORE):
                with nc.psum_tensor(_nm("trb_ps"), [128, 2048], BF16) as trb_ps:
                    xnat_free = None
                    tr_free = [None, None]
                    ev_xT_last = None
                    for tt in range(NT):
                        sl = 0
                        evc = p.dma("DG", xnat_b[:],
                                    xin[b, tt * 128:(tt + 1) * 128, :],
                                    waits=[xnat_free] + ld_const)
                        for c in range(8):
                            ts = c % 2
                            evt = p.op("T", lambda eng, sl=sl, c=c, ts=ts: eng.transpose(
                                trb_ps[:, ts * 1024: ts * 1024 + 128],
                                xnat_b[:, sl * D + c * 128: sl * D + (c + 1) * 128],
                                identb[:]),
                                waits=[evc, tr_free[ts]] + bank_gate, sig=True)
                            eve = p.op("V", lambda eng, c=c, tt=tt, ts=ts: eng.tensor_copy(
                                xT[:, c * S + tt * 128: c * S + (tt + 1) * 128],
                                trb_ps[:, ts * 1024: ts * 1024 + 128]),
                                waits=[evt], sig=True)
                            tr_free[ts] = eve
                            if c == 7:
                                xnat_free = eve
                            ev_xT_last = eve
                    bank_gate = [ev_xT_last]
                    if debug_stage == "A1" and b == 0:
                        dbg_dump("xT", xT[:], p.join_cv())
                        dbg_finish()
                        return nc
                    p.flush(block)

                with ExitStack() as cps:
                    ser_ps = cps.enter_context(nc.psum_tensor(_nm("ser_ps"), [128, 512], F32))
                    den_ps = cps.enter_context(nc.psum_tensor(_nm("den_ps"), [128, 2 * DEN_W], F32))
                    num_ps = cps.enter_context(nc.psum_tensor(_nm("num_ps"), [128, 1024], F32))
                    yT_ps = cps.enter_context(nc.psum_tensor(_nm("yT_ps"), [128, 512], F32))

                    wpkf_free = None
                    y_stage_free = [None, None]
                    qr_slot_free = [None, None]
                    den_row_free = None
                    den_slot_ev = [None, None]
                    den_acc_free = None
                    exp_slot_ev = [None, None]
                    yT_free = None
                    num_free = None
                    ser_free = None
                    y_written = [None] * NPAIR
                    prev_post = None

                    for pr in range(NPAIR + 1):
                        if pr < NPAIR:
                            psl = pr % 2
                            ysl = pr % 2
                            evw = p.dma("DG", wpk_b[:, psl * 3072:(psl + 1) * 3072],
                                        wpk_in[pr, :, :], waits=[wpkb_free[psl]])
                            for colo, dstbuf in ((0, qT), (128, kT)):
                                for cc in range(NQC):
                                    ev = None
                                    for k in range(8):
                                        ev = p.op("T", lambda eng, k=k, cc=cc, colo=colo, psl=psl: eng.matmul(
                                            ser_ps[:],
                                            wpk_b[:, psl * 3072 + k * 384 + colo: psl * 3072 + k * 384 + colo + 128],
                                            xT[:, k * S + cc * 512: k * S + cc * 512 + 512],
                                            start=(k == 0), stop=(k == 7)),
                                            waits=([evw, ser_free, ev_xT_last] + bank_gate) if k == 0 else (),
                                            sig=(k == 7))
                                    eve = p.op("V", lambda eng, cc=cc, dstbuf=dstbuf: eng.tensor_copy(
                                        dstbuf[:, cc * 512:(cc + 1) * 512], ser_ps[:]),
                                        waits=[ev], sig=True)
                                    ser_free = eve
                                if colo == 0:
                                    q_ready = eve
                                else:
                                    k_ready = eve
                            for g in range(NKT // 4):
                                ev = None
                                first = True
                                for tsub in range(4):
                                    tt = g * 4 + tsub
                                    for k in range(8):
                                        ev = p.op("T", lambda eng, k=k, tt=tt, tsub=tsub, psl=psl: eng.matmul(
                                            ser_ps[:, tsub * 128:(tsub + 1) * 128],
                                            xT[:, k * S + tt * 128: k * S + (tt + 1) * 128],
                                            wpk_b[:, psl * 3072 + k * 384 + 256: psl * 3072 + k * 384 + 384],
                                            start=(tsub == 0 and k == 0), stop=(tsub == 3 and k == 7),
                                            skip_group_check=True),
                                            waits=[evw, ser_free] if first else (),
                                            sig=(tsub == 3 and k == 7))
                                        first = False
                                eve = p.op("V", lambda eng, g=g: eng.tensor_copy(
                                    v_sb[:, g * 512:(g + 1) * 512], ser_ps[:]),
                                    waits=[ev], sig=True)
                                ser_free = eve
                            v_ready = eve
                            wpkb_free[psl] = eve
                            ev_ropek = None
                            for cc in range(NQC):
                                evsh = p.op("V", lambda eng, cc=cc: eng.stream_shuffle(
                                    qs_tmp[:], kT[:, cc * 512:(cc + 1) * 512],
                                    [i ^ 1 for i in range(32)]), waits=[k_ready],
                                    force=[k_ready], sig=True)
                                p.op("V", lambda eng, cc=cc: eng.tensor_mul(
                                    krT[:, cc * 512:(cc + 1) * 512],
                                    ctab[:, cc * 512:(cc + 1) * 512],
                                    kT[:, cc * 512:(cc + 1) * 512]))
                                p.op("V", lambda eng, cc=cc: eng.tensor_mul(
                                    rope_tmp[:], stab[:, cc * 512:(cc + 1) * 512],
                                    qs_tmp[:]), force=[evsh])
                                ev_ropek = p.op("V", lambda eng, cc=cc: eng.tensor_add(
                                    krT[:, cc * 512:(cc + 1) * 512],
                                    krT[:, cc * 512:(cc + 1) * 512],
                                    rope_tmp[:]), sig=True)

                        if pr >= 1:
                            ppr = pr - 1
                            pysl, py_evac, pden = prev_post
                            if NPD == 1:
                                evp = p.op("V", lambda eng: eng.tensor_copy(
                                    den_sum[:], den_acc[:, 0:2 * NKT]),
                                    waits=pden, sig=True)
                            else:
                                evp = p.op("V", lambda eng: eng.tensor_add(
                                    den_sum[:], den_acc[:, 0:2 * NKT],
                                    den_acc[:, 2 * NKT:4 * NKT]),
                                    waits=pden, sig=True)
                                for pc in range(2, NPD):
                                    evp = p.op("V", lambda eng, pc=pc: eng.tensor_add(
                                        den_sum[:], den_sum[:],
                                        den_acc[:, pc * 2 * NKT:(pc + 1) * 2 * NKT]),
                                        sig=True)
                            den_acc_free = evp
                            evr = p.op("V", lambda eng: eng.reciprocal(
                                den_sum[:], den_sum[:]), waits=[evp], force=[evp],
                                sig=True)
                            evtr = p.op("T", lambda eng: eng.transpose(
                                num_ps[0:2 * NKT, 0:128], den_sum[:], identf[:]),
                                waits=[evr, num_free], sig=True)
                            evte = p.op("V", lambda eng: eng.tensor_copy(
                                den_invT[0:2 * NKT, :], num_ps[0:2 * NKT, 0:128]),
                                waits=[evtr], sig=True)
                            evrows = p.dma(
                                "DS",
                                den_scr.rearrange("h (t n) -> (h t) n", n=128),
                                den_invT[0:2 * NKT, :], waits=[evte])
                            evbe_last = None
                            for cq in range(NQC):
                                evdl = [p.dma("DS", den_row_ch[0:1, h * 512:(h + 1) * 512],
                                              den_scr[h, cq * 512:(cq + 1) * 512][None, :],
                                              waits=[evrows, den_row_free])
                                        for h in range(2)]
                                evbc = None
                                for h in range(2):
                                    evbc = p.op("T", lambda eng, h=h: eng.matmul(
                                        num_ps[64 * h: 64 * h + 64, 0:512],
                                        ones_f[0:1, 0:64],
                                        den_row_ch[0:1, h * 512:(h + 1) * 512],
                                        start=True, stop=True, skip_group_check=True),
                                        waits=[evdl[1], evte, evbe_last] if h == 0 else (),
                                        sig=(h == 1))
                                den_row_free = evbc
                                evbe = p.op("V", lambda eng: eng.tensor_copy(
                                    bc_sb[:, 0:512],
                                    num_ps[:, 0:512]), waits=[evbc], sig=True)
                                evbe_last = evbe
                                evsc = p.op("V", lambda eng, cq=cq, pysl=pysl: eng.tensor_mul(
                                    y_stage[:, pysl * S + cq * 512: pysl * S + (cq + 1) * 512],
                                    y_stage[:, pysl * S + cq * 512: pysl * S + (cq + 1) * 512],
                                    bc_sb[:, 0:512]),
                                    waits=[py_evac[cq]], sig=True)
                                if cq == NQC - 1:
                                    ev_ylast = evsc
                            num_free = evbe_last
                            ev_yw = p.dma("DS", y_scr[ppr, :, :],
                                          y_stage[:, pysl * S:(pysl + 1) * S],
                                          waits=[ev_ylast])
                            y_stage_free[pysl] = ev_yw
                            y_written[ppr] = ev_yw

                        if pr < NPAIR:
                            den_units = [(h, qt, pc) for h in range(2)
                                         for qt in range(NKT) for pc in range(NPD)]
                            num_units = [(cq, kt) for cq in range(NQC)
                                         for kt in range(4 * cq + 4)]
                            nd, nn = len(den_units), len(num_units)
                            di = 0
                            y_evac = {}
                            qr_ev = [None, None]
                            pend = None

                            def emit_av(pend_, ysl=ysl):
                                nonlocal yT_free
                                cq_, kt_, esl_, evex_, evnum_ = pend_
                                lastkt_ = 4 * cq_ + 3
                                evav = None
                                for h in range(2):
                                    evav = p.op("T", lambda eng, h=h, kt_=kt_, esl_=esl_, lastkt_=lastkt_: eng.matmul(
                                        yT_ps[64 * h:64 * h + 64, :],
                                        v_sb[:, kt_ * 128 + h * 64: kt_ * 128 + h * 64 + 64],
                                        exp_sb[:, esl_ * 1024 + h * 512: esl_ * 1024 + (h + 1) * 512],
                                        start=(kt_ == 0), stop=(kt_ == lastkt_),
                                        skip_group_check=True),
                                        waits=([evex_, v_ready] + ([yT_free] if kt_ == 0 else [])) if h == 0 else (),
                                        sig=(h == 1))
                                exp_slot_ev[esl_] = evav
                                if kt_ == lastkt_:
                                    qr_slot_free[cq_ % 2] = evnum_
                                    evy = p.op("V", lambda eng, cq_=cq_, ysl=ysl: eng.tensor_copy(
                                        y_stage[:, ysl * S + cq_ * 512: ysl * S + (cq_ + 1) * 512],
                                        yT_ps[:]),
                                        waits=[evav] + ([y_stage_free[ysl]] if cq_ == 0 else []),
                                        sig=True)
                                    yT_free = evy
                                    y_evac[cq_] = evy

                            for j, (cq, kt) in enumerate(num_units):
                                qsl = cq % 2
                                if kt == 0:
                                    evsh = p.op("V", lambda eng, cq=cq, qsl=qsl: eng.stream_shuffle(
                                        qs_tmp[:], qT[:, cq * 512:(cq + 1) * 512],
                                        [i ^ 1 for i in range(32)]),
                                        waits=[q_ready, qr_slot_free[qsl]],
                                        force=[q_ready], sig=True)
                                    p.op("V", lambda eng, cq=cq, qsl=qsl: eng.tensor_mul(
                                        qr_ch[:, qsl * 512:(qsl + 1) * 512],
                                        ctab[:, cq * 512:(cq + 1) * 512],
                                        qT[:, cq * 512:(cq + 1) * 512]))
                                    p.op("V", lambda eng, cq=cq: eng.tensor_mul(
                                        rope_tmp[:], stab[:, cq * 512:(cq + 1) * 512],
                                        qs_tmp[:]), force=[evsh])
                                    qr_ev[qsl] = p.op("V", lambda eng, qsl=qsl: eng.tensor_add(
                                        qr_ch[:, qsl * 512:(qsl + 1) * 512],
                                        qr_ch[:, qsl * 512:(qsl + 1) * 512],
                                        rope_tmp[:]), sig=True)
                                dlim = ((j + 1) * nd) // nn
                                piece_evs = []
                                while di < dlim:
                                    h, qt, pc = den_units[di]
                                    dsl = di % 2
                                    ev = None
                                    for half in range(DEN_W // 512):
                                        kc = pc * (DEN_W // 512) + half
                                        ev = p.op("T", lambda eng, h=h, qt=qt, kc=kc, half=half, dsl=dsl: eng.matmul(
                                            den_ps[:, dsl * DEN_W + half * 512: dsl * DEN_W + (half + 1) * 512],
                                            qT[64 * h: 64 * h + 64, qt * 128:(qt + 1) * 128],
                                            kT[64 * h: 64 * h + 64, kc * 512:(kc + 1) * 512],
                                            start=True, stop=True),
                                            waits=[q_ready, k_ready, den_slot_ev[dsl]] if half == 0 else (),
                                            sig=(half == DEN_W // 512 - 1))
                                    col = pc * (2 * NKT) + h * NKT + qt
                                    piece_evs.append((dsl, ev, col))
                                    di += 1
                                diag = kt >= 4 * cq
                                ev = None
                                for h in range(2):
                                    ev = p.op("T", lambda eng, h=h, kt=kt, qsl=qsl: eng.matmul(
                                        num_ps[:, h * 512:(h + 1) * 512],
                                        krT[64 * h:64 * h + 64, kt * 128:(kt + 1) * 128],
                                        qr_ch[64 * h:64 * h + 64, qsl * 512:(qsl + 1) * 512],
                                        start=True, stop=not diag, skip_group_check=True),
                                        waits=[ev_ropek, qr_ev[qsl], num_free] if h == 0 else (),
                                        sig=(h == 1 and not diag))
                                    if diag:
                                        tq = kt - 4 * cq
                                        ev = p.op("T", lambda eng, h=h, tq=tq: eng.matmul(
                                            num_ps[:, h * 512:(h + 1) * 512],
                                            identb[:], masks[:, tq * 512:(tq + 1) * 512],
                                            start=False, stop=True, skip_group_check=True),
                                            sig=(h == 1))
                                evnum = ev
                                esl = j % 2
                                evex = p.op("S", lambda eng, esl=esl: eng.activation(
                                    exp_sb[:, esl * 1024:(esl + 1) * 1024], num_ps[:],
                                    AF.Exp, scale=0.125),
                                    waits=[evnum, exp_slot_ev[esl]], sig=True)
                                num_free = evex
                                for dsl, evd, col in piece_evs:
                                    evs = p.op("S", lambda eng, col=col, dsl=dsl: eng.activation(
                                        t_f32[:, 0:DEN_W], den_ps[:, dsl * DEN_W:(dsl + 1) * DEN_W],
                                        AF.Exp, scale=0.125,
                                        accum_out=den_acc[:, col:col + 1]),
                                        waits=[evd, den_acc_free], sig=True)
                                    den_slot_ev[dsl] = evs
                                if pend is not None:
                                    emit_av(pend)
                                pend = (cq, kt, esl, evex, evnum)
                            emit_av(pend)
                            prev_post = (ysl, y_evac, [den_slot_ev[0], den_slot_ev[1]])
                    bank_gate = p.join_cv()
                    if debug_stage == "pairs" and b == 0:
                        dbg_dump("qT", qT[:], p.join_cv())
                        dbg_dump("kT", kT[:], p.join_cv())
                        dbg_dump("krT", krT[:], p.join_cv())
                        dbg_dump("v", v_sb[:], p.join_cv())
                        dbg_dump("denacc", den_acc[:], p.join_cv())
                        dbg_dump("densum", den_sum[:], p.join_cv())
                        dbg_dump("deninvT", den_invT[:], p.join_cv())
                        dbg_dump("denscr", den_scr, [y_written[NPAIR - 1]])
                        dbg_dump("yscr", y_scr, [y_written[NPAIR - 1]])
                        dbg_finish()
                        return nc
                    p.flush(block)

                with nc.psum_tensor(_nm("wout_ps"), [128, 4 * 1024], F32) as wout_ps:
                    yst_free = [None, None]
                    wout_ps_free = None
                    x_f32_free = None
                    for g in range(NT // 4):
                        ev_group = None
                        for pr in range(NPAIR):
                            ysl = pr % 2
                            evl = p.dma("DA", yst_sb[:, ysl * 512:(ysl + 1) * 512],
                                        y_scr[pr, :, g * 512:(g + 1) * 512],
                                        waits=[y_written[pr], yst_free[ysl]])
                            evm = None
                            for tsub in range(4):
                                for nh in range(2):
                                    w3 = [evl, ev_woutcast] + bank_gate
                                    if pr == 0 and tsub == 0 and nh == 0:
                                        w3.append(wout_ps_free)
                                    evm = p.op("T", lambda eng, pr=pr, tsub=tsub, nh=nh, ysl=ysl: eng.matmul(
                                        wout_ps[:, tsub * 1024 + nh * 512: tsub * 1024 + (nh + 1) * 512],
                                        yst_sb[:, ysl * 512 + tsub * 128: ysl * 512 + (tsub + 1) * 128],
                                        wout_sb[:, pr * D + nh * 512: pr * D + (nh + 1) * 512],
                                        start=(pr == 0), stop=(pr == NPAIR - 1),
                                        skip_group_check=True),
                                        waits=w3 if (tsub == 0 and nh == 0) else (),
                                        sig=(tsub == 3 and nh == 1))
                            yst_free[ysl] = evm
                            ev_group = evm
                        ep_last = None
                        for tsub in range(4):
                            tt = g * 4 + tsub
                            evx = p.dma("DS", x_f32[:],
                                        xin[b, tt * 128:(tt + 1) * 128, :],
                                        waits=[x_f32_free])
                            p.op("V", lambda eng, tsub=tsub: eng.tensor_add(
                                t_f32[:], wout_ps[:, tsub * 1024:(tsub + 1) * 1024],
                                x_f32[:]),
                                waits=[ev_group, evx])
                            ev1 = p.op("V", lambda eng: eng.tensor_add(
                                t_f32[:], t_f32[:], bout_bc[:]), sig=True)
                            x_f32_free = ev1
                            _f1 = ev1
                            if debug_stage == "woutpre":
                                evw = p.dma("DS",
                                            x1_scr[b * S + tt * 128: b * S + (tt + 1) * 128, :],
                                            t_f32[:], waits=[ev1])
                                ep_last = evw
                                x1_done.append(evw)
                                ev7 = evw
                                continue
                            evrd = p.op("V", lambda eng: eng.reduce_sum(
                                stat[:, 0:1], t_f32[:], axis=AX.X), force=[_f1],
                                sig=True)
                            evmu = p.op("V", lambda eng: eng.tensor_scalar_mul(
                                stat[:, 1:2], stat[:, 0:1], 1.0 / D), force=[evrd],
                                sig=True)
                            ev2 = p.op("V", lambda eng: eng.tensor_scalar(
                                t_f32[:], t_f32[:], stat[:, 1:2], None,
                                ALU.subtract), force=[evmu], sig=True)
                            ev3 = p.op("S", lambda eng: eng.activation(
                                trash_b[:, 0:D], t_f32[:], AF.Square,
                                accum_out=stat[:, 2:3]), waits=[ev2], sig=True)
                            ev4 = ev3
                            if debug_stage == "wout" and b == 0:
                                p.op("V", lambda eng, tt=tt: eng.tensor_copy(
                                    lndbg[:, tt:tt + 1], stat[:, 2:3]), waits=[ev3])
                            ev5 = p.op("S", lambda eng: eng.activation(
                                stat[:, 4:5], stat[:, 2:3], AF.Ln,
                                scale=1.0 / D, bias=eps_sb[:, 0:1]),
                                waits=[ev_eps], force=[ev3], sig=True)
                            ev6 = p.op("S", lambda eng: eng.activation(
                                stat[:, 5:6], stat[:, 4:5], AF.Exp, scale=-0.5),
                                sig=True)
                            if debug_stage == "wout" and b == 0:
                                p.op("V", lambda eng, tt=tt: eng.tensor_copy(
                                    lndbg[:, 64 + tt:64 + tt + 1], stat[:, 5:6]),
                                    waits=[ev6])
                            ev7 = p.op("V", lambda eng: eng.tensor_scalar(
                                t_f32[:], t_f32[:], stat[:, 5:6], None, ALU.mult),
                                waits=[ev6], sig=True)
                            evw = p.dma("DS",
                                        x1_scr[b * S + tt * 128: b * S + (tt + 1) * 128, :],
                                        t_f32[:], waits=[ev7])
                            ep_last = evw
                            x1_done.append(evw)
                        wout_ps_free = ev7
                    bank_gate = p.join_cv()
                    if debug_stage in ("wout", "woutpre") and b == 0:
                        dbg_dump("x1scr", x1_scr, [x1_done[-1]])
                        if debug_stage == "wout":
                            dbg_dump("lndbg", lndbg[:], p.join_cv())
                        dbg_finish()
                        return nc
                    p.flush(block)
            ev_x1_all = list(x1_done)
            p1_gate = p.join_cv()

        with ExitStack() as c2:
            sb2 = lambda n, sh, d_: c2.enter_context(nc.sbuf_tensor(n, sh, d_))
            x1t = sb2("x1t", [128, 2 * D], F32)
            tt_f = sb2("tt_f", [128, D], F32)
            trash2 = sb2("trash2", [128, D], F32)
            stat2 = sb2("stat2", [128, 8], F32)
            xfT_f = sb2("xfT_f", [128, 8 * 128], F32)
            xfn_b4 = sb2("xfn_b4", [128, 8 * 512], BF16)
            logit_sb = sb2("logit_sb", [128, 8], F32)
            eq_sb = sb2("eq_sb", [128, 8], F32)
            we8 = sb2("we8", [128, 8], F32)
            weT_all = sb2("weT_all", [8, TOK], F32)

            with ExitStack() as cp2:
                trf_ps = cp2.enter_context(nc.psum_tensor(_nm("trf_ps"), [128, 2 * 512], F32))
                log_ps = cp2.enter_context(nc.psum_tensor(_nm("log_ps"), [128, 8], F32))
                tr8_ps = cp2.enter_context(nc.psum_tensor(_nm("tr8_ps"), [128, 128], F32))

                x1t_free = [None, None]
                trf_free = [None, None]
                log_free = None
                tr8_free = None
                stage_ev = None
                for tt in range(NTT):
                    sl = tt % 2
                    evx = p.dma("DS", x1t[:, sl * D:(sl + 1) * D],
                                x1_scr[tt * 128:(tt + 1) * 128, :],
                                waits=[x1t_free[sl]] + (ev_x1_all[-1:] if tt == 0 else []))
                    evrd = p.op("V", lambda eng, sl=sl: eng.reduce_sum(
                        stat2[:, 0:1], x1t[:, sl * D:(sl + 1) * D], axis=AX.X),
                        waits=[evx] + p1_gate, sig=True)
                    evmu = p.op("V", lambda eng: eng.tensor_scalar_mul(
                        stat2[:, 1:2], stat2[:, 0:1], 1.0 / D), force=[evrd],
                        sig=True)
                    ev2 = p.op("V", lambda eng, sl=sl: eng.tensor_scalar(
                        tt_f[:], x1t[:, sl * D:(sl + 1) * D], stat2[:, 1:2], None,
                        ALU.subtract), force=[evmu], sig=True)
                    x1t_free[sl] = ev2
                    ev3 = p.op("S", lambda eng: eng.activation(
                        trash2[:], tt_f[:], AF.Square, accum_out=stat2[:, 2:3]),
                        waits=[ev2] + p1_gate, sig=True)
                    ev4 = ev3
                    ev5 = p.op("S", lambda eng: eng.activation(
                        stat2[:, 4:5], stat2[:, 2:3], AF.Ln,
                        scale=1.0 / D, bias=eps_sb[:, 0:1]),
                        waits=[ev_eps], force=[ev3], sig=True)
                    ev6 = p.op("S", lambda eng: eng.activation(
                        stat2[:, 5:6], stat2[:, 4:5], AF.Exp, scale=-0.5), sig=True)
                    p.op("V", lambda eng: eng.tensor_scalar(
                        tt_f[:], tt_f[:], stat2[:, 5:6], None, ALU.mult),
                        waits=[ev6])
                    p.op("V", lambda eng: eng.tensor_mul(tt_f[:], tt_f[:], lnw_bc[:]))
                    ev7 = p.op("V", lambda eng: eng.tensor_add(
                        tt_f[:], tt_f[:], lnb_bc[:]), sig=True)
                    evlog = None
                    for c in range(8):
                        ts = c % 2
                        evt = p.op("T", lambda eng, c=c, ts=ts: eng.transpose(
                            trf_ps[:, ts * 512: ts * 512 + 128],
                            tt_f[:, c * 128:(c + 1) * 128], identf[:]),
                            waits=[ev7, trf_free[ts]] + (p1_gate if tt == 0 else []),
                            sig=True)
                        evcp = p.op("V", lambda eng, c=c, ts=ts: eng.tensor_copy(
                            xfT_f[:, c * 128:(c + 1) * 128],
                            trf_ps[:, ts * 512: ts * 512 + 128]),
                            waits=[evt], sig=True)
                        evb = p.op("V", lambda eng, c=c, tt=tt: eng.tensor_copy(
                            xfn_b4[:, c * 512 + (tt % 4) * 128: c * 512 + (tt % 4 + 1) * 128],
                            xfT_f[:, c * 128:(c + 1) * 128]),
                            waits=[stage_ev if tt % 4 == 0 else None], sig=True)
                        trf_free[ts] = evb
                        evlog = p.op("T", lambda eng, c=c: eng.matmul(
                            log_ps[:], xfT_f[:, c * 128:(c + 1) * 128],
                            routsb[:, c * 8:(c + 1) * 8],
                            start=(c == 0), stop=(c == 7)),
                            waits=[evcp] + ([log_free] if c == 0 else []),
                            sig=(c == 7))
                    if tt % 4 == 3:
                        evst = [p.dma("DS",
                                      xfn_scr[c, :, (tt - 3) * 128:(tt + 1) * 128],
                                      xfn_b4[:, c * 512:(c + 1) * 512],
                                      waits=[evb]) for c in range(8)]
                        stage_ev = evst[-1]
                    evlc = p.op("V", lambda eng: eng.tensor_copy(
                        logit_sb[:], log_ps[:]), waits=[evlog], sig=True)
                    log_free = evlc
                    evrm = p.op("V", lambda eng: eng.reduce_max(
                        stat2[:, 6:7], logit_sb[:], axis=AX.X), force=[evlc],
                        sig=True)
                    ev8 = p.op("V", lambda eng: eng.tensor_scalar(
                        eq_sb[:], logit_sb[:], stat2[:, 6:7], None, ALU.is_equal),
                        force=[evrm], sig=True)
                    ev9 = p.op("S", lambda eng: eng.activation(
                        stat2[:, 7:8], stat2[:, 6:7], AF.Exp, scale=-1.0),
                        waits=[ev8], sig=True)
                    ev9b = p.op("V", lambda eng: eng.tensor_scalar_add(
                        stat2[:, 7:8], stat2[:, 7:8], 1.0), waits=[ev9], sig=True)
                    ev10 = p.op("V", lambda eng: eng.reciprocal(
                        stat2[:, 7:8], stat2[:, 7:8]), force=[ev9b], sig=True)
                    ev11 = p.op("V", lambda eng: eng.tensor_scalar(
                        we8[:], eq_sb[:], stat2[:, 7:8], None, ALU.mult),
                        force=[ev10], sig=True)
                    evt8 = p.op("T", lambda eng: eng.transpose(
                        tr8_ps[0:8, :], we8[:], identf[:]),
                        waits=[ev11, tr8_free], sig=True)
                    tr8_free = p.op("V", lambda eng, tt=tt: eng.tensor_copy(
                        weT_all[:, tt * 128:(tt + 1) * 128], tr8_ps[0:8, 0:128]),
                        waits=[evt8], sig=True)
                ev_we_all = p.dma("DS", we_scr, weT_all[:], waits=[tr8_free])
                if debug_stage == "p2":
                    dbg_dump("weT", weT_all[:], p.join_cv())
                    dbg_dump("logit", logit_sb[:], p.join_cv())
                    dbg_dump("eq", eq_sb[:], p.join_cv())
                    dbg_dump("we8", we8[:], p.join_cv())
                    dbg_dump("stat2", stat2[:], p.join_cv())
                    dbg_finish()
                    return nc
                p2_gate = p.join_cv()
                p.flush(block)
        c0.close()

        with ExitStack() as c3:
            sb3 = lambda n, sh, d_: c3.enter_context(nc.sbuf_tensor(n, sh, d_))
            ET = 512
            NQ3 = TOK // ET
            ETT3 = ET // 128
            xfn_sb = sb3("xfn_sb", [128, 8 * ET], BF16)
            xfe_sb = sb3("xfe_sb", [128, 8 * ET], BF16)
            hmid = sb3("hmid", [128, 2 * 16 * ET], BF16)
            g_full = sb3("g_full", [128, 8 * I_FF], BF16)
            u_full = sb3("u_full", [128, 8 * I_FF], BF16)
            d_full = sb3("d_full", [128, 16 * D], BF16)
            rin_sb = sb3("rin_sb", [128, ETT3 * D], BF16)
            rout_sb = sb3("rout_sb", [128, ETT3 * D], BF16)
            radd_sb = sb3("radd_sb", [128, D], F32)
            bc3_sb = sb3("bc3_sb", [128, ET], F32)
            silu_sb = sb3("silu_sb", [128, 2 * 512], BF16)
            x1f = sb3("x1f", [128, D], F32)
            tf4 = sb3("tf4", [128, D], F32)
            stat4 = sb3("stat4", [128, 8], F32)

            with ExitStack() as cp3:
                hg_ps = cp3.enter_context(nc.psum_tensor(_nm("hg_ps"), [128, 2 * 512], F32))
                hu_ps = cp3.enter_context(nc.psum_tensor(_nm("hu_ps"), [128, 2 * 512], F32))
                rt_ps = cp3.enter_context(nc.psum_tensor(_nm("rt_ps"), [128, 2 * 1024], F32))

                order = [NE] + list(range(NE))
                g_last = u_last = d_last = None
                store_ev = [None] * NQ3
                new_store = [None] * NQ3
                hg_free = [None, None]
                hu_free = [None, None]
                silu_free = [None, None]
                hmslot_free = [None, None]
                xfe_free = None
                xfn_free = None
                rt_free = [None, None]
                bc3_sb_free = None
                rin_free = None
                rin_ev = None
                rout_free = None
                radd_free = None
                x1f_free = None
                tf4_free = None

                for ei, e in enumerate(order):
                    first_e = ei == 0
                    last_e = ei == NE
                    wg0 = (p2_gate + [ev_we_all, stage_ev]) if first_e else []
                    gevs = [p.dma("DG",
                                  g_full.rearrange("p (k i) -> p k i", i=I_FF)[
                                      :, :, mg * 512:(mg + 1) * 512],
                                  gate_in[e].rearrange("(k p) i -> p k i", p=128)[
                                      :, :, mg * 512:(mg + 1) * 512],
                                  waits=[g_last] + wg0) for mg in range(4)]
                    uevs = [p.dma("DG",
                                  u_full.rearrange("p (k i) -> p k i", i=I_FF)[
                                      :, :, mg * 512:(mg + 1) * 512],
                                  up_in[e].rearrange("(k p) i -> p k i", p=128)[
                                      :, :, mg * 512:(mg + 1) * 512],
                                  waits=[u_last] + wg0) for mg in range(4)]
                    devs = [p.dma("DG",
                                  d_full.rearrange("p (m n) -> p m n", n=D)[
                                      :, h * 8:(h + 1) * 8, :],
                                  down_in[e].rearrange("(m p) n -> p m n", p=128)[
                                      :, h * 8:(h + 1) * 8, :],
                                  waits=[d_last] + wg0) for h in range(2)]

                    pend = None
                    ev_gu_chunk_last = [None, None]

                    def emit_down(qq, hm_ev):
                        nonlocal rout_free, radd_free, x1f_free, tf4_free
                        nonlocal rin_free, d_last_cand
                        dslot = qq % 2
                        evd = None
                        ev_last = None
                        for tt in range(ETT3):
                            rsl = tt % 2
                            gt = qq * ETT3 + tt
                            b4 = gt // NT
                            r04 = (gt % NT) * 128
                            for m in range(16):
                                for nh in range(2):
                                    evd = p.op("T", lambda eng, m=m, tt=tt, nh=nh, dslot=dslot, rsl=rsl: eng.matmul(
                                        rt_ps[:, rsl * 1024 + nh * 512: rsl * 1024 + (nh + 1) * 512],
                                        hmid[:, dslot * 8192 + m * ET + tt * 128: dslot * 8192 + m * ET + (tt + 1) * 128],
                                        d_full[:, m * D + nh * 512: m * D + (nh + 1) * 512],
                                        start=(m == 0), stop=(m == 15),
                                        skip_group_check=True),
                                        waits=([hm_ev, devs[0], devs[1], rt_free[rsl]] +
                                               (p2_gate if first_e and qq == 0 and tt == 0 else []))
                                        if (m == 0 and nh == 0) else (),
                                        sig=(m == 15 and nh == 1))
                            if first_e:
                                evr = p.op("V", lambda eng, tt=tt, rsl=rsl: eng.tensor_copy(
                                    rout_sb[:, tt * D:(tt + 1) * D],
                                    rt_ps[:, rsl * 1024:(rsl + 1) * 1024]),
                                    waits=[evd, rout_free], sig=True)
                                rt_free[rsl] = evr
                            elif not last_e:
                                evr = p.op("V", lambda eng, tt=tt, rsl=rsl: eng.tensor_add(
                                    rout_sb[:, tt * D:(tt + 1) * D],
                                    rin_sb[:, tt * D:(tt + 1) * D],
                                    rt_ps[:, rsl * 1024:(rsl + 1) * 1024]),
                                    waits=[evd, rin_ev, rout_free], sig=True)
                                rt_free[rsl] = evr
                            else:
                                evra = p.op("V", lambda eng, tt=tt, rsl=rsl: eng.tensor_add(
                                    radd_sb[:], rin_sb[:, tt * D:(tt + 1) * D],
                                    rt_ps[:, rsl * 1024:(rsl + 1) * 1024]),
                                    waits=[evd, rin_ev, radd_free], sig=True)
                                rt_free[rsl] = evra
                                evx = p.dma("DA", x1f[:],
                                            x1_scr[gt * 128:(gt + 1) * 128, :],
                                            waits=[x1f_free, ev_x1_all[-1]])
                                ev1 = p.op("V", lambda eng: eng.scalar_tensor_tensor(
                                    tf4[:], x1f[:], 2.0, radd_sb[:],
                                    ALU.mult, ALU.add),
                                    waits=[evra, evx, tf4_free], sig=True)
                                x1f_free = ev1
                                evrd = p.op("V", lambda eng: eng.reduce_sum(
                                    stat4[:, 0:1], tf4[:], axis=AX.X),
                                    force=[ev1], sig=True)
                                evmu = p.op("V", lambda eng: eng.tensor_scalar_mul(
                                    stat4[:, 1:2], stat4[:, 0:1], 1.0 / D),
                                    force=[evrd], sig=True)
                                ev2 = p.op("V", lambda eng: eng.tensor_scalar(
                                    tf4[:], tf4[:], stat4[:, 1:2], None,
                                    ALU.subtract), force=[evmu], sig=True)
                                ev3 = p.op("S", lambda eng: eng.activation(
                                    radd_sb[:], tf4[:], AF.Square,
                                    accum_out=stat4[:, 2:3]),
                                    waits=[ev2], sig=True)
                                radd_free = ev3
                                ev5 = p.op("S", lambda eng: eng.activation(
                                    stat4[:, 4:5], stat4[:, 2:3], AF.Ln,
                                    scale=1.0 / D, bias=eps_sb[:, 0:1]),
                                    waits=[ev_eps], force=[ev3], sig=True)
                                ev6 = p.op("S", lambda eng: eng.activation(
                                    stat4[:, 5:6], stat4[:, 4:5], AF.Exp,
                                    scale=-0.5), force=[ev5], sig=True)
                                ev7 = p.op("V", lambda eng: eng.tensor_scalar(
                                    tf4[:], tf4[:], stat4[:, 5:6], None,
                                    ALU.mult), waits=[ev6], sig=True)
                                tf4_free = p.dma("DS", out[b4, r04:r04 + 128, :],
                                                 tf4[:], waits=[ev7])
                            ev_last = rt_free[rsl]
                        if not last_e:
                            evst = p.dma(
                                "DS",
                                routed_scr[qq * ET:(qq + 1) * ET, :].rearrange(
                                    "(t p) d -> p t d", p=128),
                                rout_sb[:].rearrange("p (t d) -> p t d", d=D),
                                waits=[ev_last])
                            rout_free = evst
                            new_store[qq] = evst
                        if not first_e:
                            rin_free = ev_last
                        hmslot_free[dslot] = evd
                        d_last_cand = evd

                    d_last_cand = None
                    for q in range(NQ3):
                        xsl = q % 2
                        if first_e:
                            lev = None
                            for c in range(8):
                                lev = p.dma("DS",
                                            xfe_sb[:, c * ET:(c + 1) * ET],
                                            xfn_scr[c, :, q * ET:(q + 1) * ET],
                                            waits=[xfe_free, stage_ev] +
                                                  (p2_gate if q < 2 else []))
                            src_ev = lev
                        else:
                            lev = None
                            for c in range(8):
                                lev = p.dma("DS",
                                            xfn_sb[:, c * ET:(c + 1) * ET],
                                            xfn_scr[c, :, q * ET:(q + 1) * ET],
                                            waits=[xfn_free, stage_ev])
                            evb2 = p.dma(
                                "DS", bc3_sb[:],
                                we_scr[e:e + 1, q * ET:(q + 1) * ET]
                                .partition_broadcast(128),
                                waits=[ev_we_all, bc3_sb_free] +
                                      (p2_gate if ei == 1 and q == 0 else []))
                            src_ev = None
                            for c in range(8):
                                src_ev = p.op("V", lambda eng, c=c: eng.tensor_mul(
                                    xfe_sb[:, c * ET:(c + 1) * ET],
                                    xfn_sb[:, c * ET:(c + 1) * ET],
                                    bc3_sb[:]),
                                    waits=[evb2, lev, xfe_free] if c == 0 else (),
                                    sig=(c == 7))
                            bc3_sb_free = src_ev
                            xfn_free = src_ev
                        hm_last = None
                        evh = evu2 = None
                        for mb in range(16):
                            hsl = mb % 2
                            for k in range(8):
                                evh = p.op("T", lambda eng, k=k, mb=mb, hsl=hsl: eng.matmul(
                                    hg_ps[:, hsl * 512:(hsl + 1) * 512],
                                    g_full[:, k * I_FF + mb * 128: k * I_FF + (mb + 1) * 128],
                                    xfe_sb[:, k * ET: k * ET + 512],
                                    start=(k == 0), stop=(k == 7)),
                                    waits=([gevs[mb // 4], src_ev, hg_free[hsl]] +
                                           (p2_gate if first_e and q == 0 and mb == 0 else []))
                                    if k == 0 else (),
                                    sig=(k == 7))
                            for k in range(8):
                                evu2 = p.op("T", lambda eng, k=k, mb=mb, hsl=hsl: eng.matmul(
                                    hu_ps[:, hsl * 512:(hsl + 1) * 512],
                                    u_full[:, k * I_FF + mb * 128: k * I_FF + (mb + 1) * 128],
                                    xfe_sb[:, k * ET: k * ET + 512],
                                    start=(k == 0), stop=(k == 7)),
                                    waits=[uevs[mb // 4], src_ev, hu_free[hsl]]
                                    if k == 0 else (),
                                    sig=(k == 7))
                            evsl = p.op("S", lambda eng, hsl=hsl: eng.activation(
                                silu_sb[:, hsl * 512:(hsl + 1) * 512],
                                hg_ps[:, hsl * 512:(hsl + 1) * 512], AF.Silu),
                                waits=[evh, silu_free[hsl]], sig=True)
                            hg_free[hsl] = evsl
                            evhm = p.op("V", lambda eng, mb=mb, hsl=hsl, q=q: eng.tensor_mul(
                                hmid[:, (q % 2) * 8192 + mb * ET: (q % 2) * 8192 + (mb + 1) * ET],
                                silu_sb[:, hsl * 512:(hsl + 1) * 512],
                                hu_ps[:, hsl * 512:(hsl + 1) * 512]),
                                waits=[evsl, evu2] +
                                      ([hmslot_free[q % 2]] if mb == 0 else []),
                                sig=True)
                            hu_free[hsl] = evhm
                            silu_free[hsl] = evhm
                            hm_last = evhm
                        xfe_free = evu2
                        ev_gu_chunk_last = [evh, evu2]
                        if pend is not None:
                            emit_down(*pend)
                        pend = (q, hm_last)
                        if not first_e:
                            rin_ev = p.dma(
                                "DA",
                                rin_sb[:].rearrange("p (t d) -> p t d", d=D),
                                routed_scr[q * ET:(q + 1) * ET, :].rearrange(
                                    "(t p) d -> p t d", p=128),
                                waits=[store_ev[q], rin_free])
                    emit_down(*pend)
                    g_last, u_last = ev_gu_chunk_last
                    d_last = d_last_cand
                    store_ev = list(new_store)
            p.op("SY", lambda eng: eng.nop(),
                 waits=[(ch, p.count[ch]) for ch in DMA_CH if p.count[ch] > 0])
            p.flush(block)
    return nc


def _pack_inputs(ins, S=4096):
    f32 = np.float32
    bf16 = ml_dtypes.bfloat16
    x = np.ascontiguousarray(np.asarray(ins["L0_attn_x"], f32))
    w_qkv = np.asarray(ins["L0_attn_w_qkv"], f32)
    wqkvT = np.ascontiguousarray(w_qkv.T)
    wpk = np.empty((NPAIR, 128, 8 * 384), f32)
    for pr in range(NPAIR):
        t = np.concatenate([wqkvT[:, 128 * pr:128 * pr + 128],
                            wqkvT[:, D + 128 * pr:D + 128 * pr + 128],
                            wqkvT[:, 2 * D + 128 * pr:2 * D + 128 * pr + 128]],
                           axis=1)
        wpk[pr] = t.reshape(8, 128, 384).transpose(1, 0, 2).reshape(128, 8 * 384)
    w_out = np.asarray(ins["L0_attn_w_out"], f32)
    woutpk = np.ascontiguousarray(
        w_out.T.reshape(8, 128, D).transpose(1, 0, 2).reshape(128, 8 * D))
    router = np.asarray(ins["L0_ffn_router_w"], f32)
    routpk = np.ascontiguousarray(
        router.reshape(8, 128, 8).transpose(1, 0, 2).reshape(128, 64))
    rows = np.stack([np.asarray(ins["L0_attn_b_out"], f32),
                     np.asarray(ins["L0_ffn_ln_w"], f32),
                     np.asarray(ins["L0_ffn_ln_b"], f32)], axis=0)
    i_idx = (np.arange(128) % 64) // 2
    theta = 10000.0 ** (-2.0 * (i_idx - 1.0) / 64.0)
    tpos = np.arange(S, dtype=np.float64)
    ang = tpos[None, :] * theta[:, None]
    ctab = np.cos(ang).astype(bf16)
    sgn = np.where(np.arange(128) % 2 == 0, 1.0, -1.0)[:, None]
    stab = (np.sin(ang) * sgn).astype(bf16)
    pgrid = np.arange(128)[:, None]
    ngrid = np.arange(512)[None, :]
    masks = np.concatenate(
        [np.where(ngrid >= 128 * tq + pgrid, 0.0, NEG) for tq in range(4)],
        axis=1).astype(bf16)
    identb = np.eye(128, dtype=bf16)
    identf = np.eye(128, dtype=f32)
    gate = np.concatenate([np.asarray(ins["L0_ffn_gate_w"], f32),
                           np.asarray(ins["L0_ffn_sh_gate"], f32)[None]], axis=0)
    up = np.concatenate([np.asarray(ins["L0_ffn_up_w"], f32),
                         np.asarray(ins["L0_ffn_sh_up"], f32)[None]], axis=0)
    down = np.concatenate([np.asarray(ins["L0_ffn_down_w"], f32),
                           np.asarray(ins["L0_ffn_sh_down"], f32)[None]], axis=0)
    shared = {"wpk_in": wpk, "woutpk": woutpk, "routpk": routpk, "rows_in": rows,
              "ctab_in": np.ascontiguousarray(ctab),
              "stab_in": np.ascontiguousarray(stab),
              "masks_in": np.ascontiguousarray(masks),
              "identb_in": identb, "identf_in": identf,
              "gate_in": np.ascontiguousarray(gate),
              "up_in": np.ascontiguousarray(up),
              "down_in": np.ascontiguousarray(down)}
    n_b = x.shape[0]
    in_maps = []
    for c in range(n_b // B_CORE):
        m = dict(shared)
        m["xin"] = np.ascontiguousarray(x[c * B_CORE:(c + 1) * B_CORE])
        in_maps.append(m)
    return in_maps


def kernel(**inputs):
    S = int(np.asarray(inputs["L0_attn_x"]).shape[1])
    in_maps = _pack_inputs(inputs, S=S)
    nc = build_core_program(S=S)
    res = run_bass_kernel_spmd(nc, in_maps, core_ids=list(range(len(in_maps))))
    outs = [np.asarray(r["out"], dtype=np.float32) for r in res.results]
    return np.concatenate(outs, axis=0)
